# version 11 (final)
# baseline (speedup 1.0000x reference)
import contextlib
import numpy as np
import ml_dtypes
import concourse.bass as bass
import concourse.mybir as mybir
from concourse.bass_utils import run_bass_kernel_spmd

F32 = mybir.dt.float32
BF16 = mybir.dt.bfloat16
AF = mybir.ActivationFunctionType
ALU = mybir.AluOpType
AX = mybir.AxisListType

D = 2048
H = 16
OWN = 2048
NT = 16
FQKV = 6144
EPS = 1e-5
SCALE = float(D) ** (-0.25)

DEBUG_TAPS = ()


def build_program(taps=()):
    nc = bass.Bass("TRN2", target_bir_lowering=False, debug=False)

    x_sh = nc.dram_tensor("x_sh", [OWN, D], F32, kind="ExternalInput").ap()
    WqkvT = nc.dram_tensor("WqkvT", [D, FQKV], F32, kind="ExternalInput").ap()
    bqkv = nc.dram_tensor("bqkv", [1, FQKV], F32, kind="ExternalInput").ap()
    WoutTd = nc.dram_tensor("WoutTd", [D, D], F32, kind="ExternalInput").ap()
    bout = nc.dram_tensor("bout", [1, D], F32, kind="ExternalInput").ap()
    sin2d = nc.dram_tensor("sin2d", [128, 2 * D], F32, kind="ExternalInput").ap()
    cos2d = nc.dram_tensor("cos2d", [128, 2 * D], F32, kind="ExternalInput").ap()
    masksd = nc.dram_tensor("masksd", [4, 128, 128], BF16, kind="ExternalInput").ap()
    identd = nc.dram_tensor("identd", [128, 128], BF16, kind="ExternalInput").ap()
    onesd = nc.dram_tensor("onesd", [128, 128], BF16, kind="ExternalInput").ap()
    WcatKd = nc.dram_tensor("WcatKd", [D, 144], F32, kind="ExternalInput").ap()
    Gmatd = nc.dram_tensor("Gmatd", [72, D], F32, kind="ExternalInput").ap()
    slopd = nc.dram_tensor("slopd", [128, 8], F32, kind="ExternalInput").ap()
    centd = nc.dram_tensor("centd", [128, 8], F32, kind="ExternalInput").ap()

    y = nc.dram_tensor("y", [OWN, D], F32, kind="ExternalOutput").ap()
    tap_out = {}
    if "qkv" in taps:
        tap_out["qkv"] = nc.dram_tensor("t_qkv", [NT, 128, FQKV], BF16,
                                        kind="ExternalOutput").ap()
    if "gath" in taps:
        tap_out["gath"] = nc.dram_tensor("t_gath", [NT, 4, 2, H, 128, 128], BF16,
                                         kind="ExternalOutput").ap()
    if "at" in taps:
        tap_out["at"] = nc.dram_tensor("t_at", [128, 32768], BF16,
                                       kind="ExternalOutput").ap()

    qkv_raw = nc.dram_tensor("qkv_raw", [NT, 128, FQKV], BF16).ap()
    shardK = nc.dram_tensor("shardK", [NT, H, 128, 128], BF16).ap()
    shardV = nc.dram_tensor("shardV", [NT, H, 128, 128], BF16).ap()
    gathK = nc.dram_tensor("gathK", [NT, 4, H, 128, 128], BF16).ap()
    gathV = nc.dram_tensor("gathV", [NT, 4, H, 128, 128], BF16).ap()
    crow_d = nc.dram_tensor("crow_d", [136], F32).ap()

    groups = [[0, 1, 2, 3], [4, 5, 6, 7]]

    ctx = contextlib.ExitStack()
    with ctx:
        arenaA = ctx.enter_context(nc.sbuf_tensor("arenaA", [128, 32768], BF16))
        id_s = ctx.enter_context(nc.sbuf_tensor("id_s", [128, 128], BF16))
        ones_s = ctx.enter_context(nc.sbuf_tensor("ones_s", [128, 128], BF16))
        wcat_s = ctx.enter_context(nc.sbuf_tensor("wcat_s", [128, 16 * 144], BF16))
        g_s = ctx.enter_context(nc.sbuf_tensor("g_s", [72, D], BF16))
        bo_bf = ctx.enter_context(nc.sbuf_tensor("bo_bf", [1, D], BF16))
        slope_s = ctx.enter_context(nc.sbuf_tensor("slope_s", [128, 8], F32))
        center_s = ctx.enter_context(nc.sbuf_tensor("center_s", [128, 8], F32))
        ccu_b = ctx.enter_context(nc.sbuf_tensor("ccu_b", [128, 128], F32))
        mm_b = ctx.enter_context(nc.sbuf_tensor("mm_b", [128, 8], F32))
        s1 = ctx.enter_context(nc.sbuf_tensor("s1", [128, 1], F32))
        ssq = ctx.enter_context(nc.sbuf_tensor("ssq", [128, 1], F32))
        mu = ctx.enter_context(nc.sbuf_tensor("mu", [128, 1], F32))
        var = ctx.enter_context(nc.sbuf_tensor("var", [128, 1], F32))
        rstd = ctx.enter_context(nc.sbuf_tensor("rstd", [128, 1], F32))
        tmp1 = ctx.enter_context(nc.sbuf_tensor("tmp1", [128, 1], F32))
        ssh2 = ctx.enter_context(nc.sbuf_tensor("ssh2", [128, 1], F32))
        sq_dump = ctx.enter_context(nc.sbuf_tensor("sq_dump", [128, D], F32))

        block = ctx.enter_context(nc.Block())

        SEM, C = {}, {}
        for name in (["sv", "sa", "sg", "pm", "pms", "pmv", "xld0", "xld1",
                      "wld0", "wld1", "qst0", "qst1", "qst2", "qst3",
                      "qld0", "qld1", "shk", "shv",
                      "w2ld", "ost", "pre", "preg", "scl"]
                     + [f"kld{i}" for i in range(8)]
                     + [f"vld{i}" for i in range(8)]
                     + [f"cc{i}" for i in range(16)]
                     + [f"cv{i}" for i in range(16)]):
            SEM[name] = ctx.enter_context(nc.semaphore(name))
            C[name] = 0

        def bump(inst, name, v=1):
            inst.then_inc(SEM[name], v)
            C[name] += v
            return C[name]

        def sop(e, sname, emit, also=()):
            for (nm, val) in also:
                if val:
                    e.wait_ge(SEM[nm], val)
            e.wait_ge(SEM[sname], C[sname])
            return bump(emit(), sname)

        V = block.vector
        A = block.scalar
        G = block.gpsimd
        T = block.tensor
        SY = block.sync

        from concourse.hw_specs import get_activation_tables
        _setidx = list(get_activation_tables(nc.m.arch).keys()).index(
            "natural_log_exp_and_others")

        def p0_act(a):
            a.add_instruction(mybir.InstLoadActFuncSet(
                name=nc.get_next_instruction_name(),
                act_func_set_id=_setidx, ins=[], outs=[]))
        A(p0_act)

        def p0_loads(s):
            for dst, src in [(id_s[:], identd), (ones_s[:], onesd),
                             (slope_s[:], slopd), (center_s[:], centd)]:
                s.dma_start(out=dst, in_=src).then_inc(SEM["pre"], 16)
                C["pre"] += 16
        SY(p0_loads)

        def p0_cast(g):
            g.dma_start(out=wcat_s[:].rearrange("p (c f) -> p c f", c=16),
                        in_=WcatKd.rearrange("(c p) f -> p c f", p=128)
                        ).then_inc(SEM["preg"], 16)
            g.dma_start(out=g_s[:], in_=Gmatd).then_inc(SEM["preg"], 16)
            g.dma_start(out=bo_bf[:], in_=bout).then_inc(SEM["preg"], 16)
            C["preg"] += 48
        G(p0_cast)
        pre_all = C["pre"]
        preg_all = C["preg"]

        pre_ps = contextlib.ExitStack()
        with pre_ps:
            ps_c = pre_ps.enter_context(nc.psum_tensor("ps_c", [128, 512], F32))
            ps_cu = pre_ps.enter_context(nc.psum_tensor("ps_cu", [128, 512], F32))
            ps_mm = pre_ps.enter_context(nc.psum_tensor("ps_mm", [128, 512], F32))
            crow_s = pre_ps.enter_context(nc.sbuf_tensor("crow_s", [1, 144], F32))

            def p0_ffnconst(t):
                t.wait_ge(SEM["pre"], pre_all)
                t.wait_ge(SEM["preg"], preg_all)
                first = [True, True, True]
                for n in range(8):
                    for c in range(16):
                        base = c * 144
                        last = (n == 7 and c == 15)
                        t.matmul(ps_c[0:1, n * 8:(n + 1) * 8],
                                 wcat_s[:, base + 128 + n:base + 129 + n],
                                 wcat_s[:, base + n * 8:base + (n + 1) * 8],
                                 start=first[0], stop=last)
                        t.matmul(ps_cu[0:1, n * 8:(n + 1) * 8],
                                 wcat_s[:, base + 136 + n:base + 137 + n],
                                 wcat_s[:, base + 64 + n * 8:base + 64 + (n + 1) * 8],
                                 start=first[1], stop=last)
                        mi = t.matmul(ps_mm[0:1, n:n + 1],
                                      wcat_s[:, base + 128 + n:base + 129 + n],
                                      wcat_s[:, base + 128 + n:base + 129 + n],
                                      start=first[2], stop=last)
                        first = [False, False, False]
                mi.then_inc(SEM["pm"], 1)
                C["pm"] += 1
            T(p0_ffnconst)
            pm_ffn = C["pm"]

            def p0_evac(v):
                sop(v, "sv", lambda: v.tensor_copy(crow_s[0:1, 0:64], ps_c[0:1, 0:64]),
                    also=[("pm", pm_ffn)])
                sop(v, "sv", lambda: v.tensor_copy(crow_s[0:1, 64:128], ps_cu[0:1, 0:64]))
                sop(v, "sv", lambda: v.tensor_copy(crow_s[0:1, 128:136], ps_mm[0:1, 0:8]))
            V(p0_evac)
            sv_crow = C["sv"]

            def p0_bounce(s):
                s.wait_ge(SEM["sv"], sv_crow)
                s.dma_start(out=crow_d.unsqueeze(0), in_=crow_s[0:1, 0:136]
                            ).then_inc(SEM["pre"], 16)
                C["pre"] += 16
                s.wait_ge(SEM["pre"], C["pre"])
                s.dma_start(out=ccu_b[:],
                            in_=crow_d[0:128].unsqueeze(0).broadcast_to([128, 128])
                            ).then_inc(SEM["pre"], 16)
                s.dma_start(out=mm_b[:],
                            in_=crow_d[128:136].unsqueeze(0).broadcast_to([128, 8])
                            ).then_inc(SEM["pre"], 16)
                C["pre"] += 32
            SY(p0_bounce)
            pre_done = C["pre"]

        p1a = contextlib.ExitStack()
        with p1a:
            xs0 = p1a.enter_context(nc.sbuf_tensor("xs0", [128, D], F32))
            xs1 = p1a.enter_context(nc.sbuf_tensor("xs1", [128, D], F32))
            h_bf = p1a.enter_context(nc.sbuf_tensor("h_bf", [128, D], BF16))
            ptr0 = p1a.enter_context(nc.psum_tensor("ptr0", [128, 1024], BF16))
            ptr1 = p1a.enter_context(nc.psum_tensor("ptr1", [128, 1024], BF16))
            xbufs = [xs0, xs1]
            ptrs = [ptr0, ptr1]

            sv_affine_done = []
            tr_evac = [None, None]
            for l in range(NT):
                xb = xbufs[l % 2]

                xsem = f"xld{l % 2}"

                def p1a_load(s, l=l, xb=xb, xsem=xsem):
                    if l >= 2:
                        s.wait_ge(SEM["sv"], sv_affine_done[l - 2])
                    s.dma_start(out=xb[:], in_=x_sh[l * 128:(l + 1) * 128, :]
                                ).then_inc(SEM[xsem], 16)
                    C[xsem] += 16
                SY(p1a_load)
                xld_t = C[xsem]

                def p1a_sum(v, xb=xb, xld_t=xld_t, xsem=xsem):
                    sop(v, "sv", lambda: v.reduce_sum(s1[:], xb[:], axis=AX.X),
                        also=[(xsem, xld_t)])
                    sop(v, "sv", lambda: v.tensor_scalar(mu[:], s1[:], 1.0 / D, None, ALU.mult))
                V(p1a_sum)

                def p1a_sq(a, xb=xb, xld_t=xld_t, xsem=xsem):
                    sop(a, "sa", lambda: a.activation(sq_dump[:], xb[:], AF.Square,
                                                      accum_out=ssq[:]),
                        also=[(xsem, xld_t)])
                A(p1a_sq)
                sa_ss = C["sa"]

                def p1a_var(v, sa_ss=sa_ss):
                    sop(v, "sv", lambda: v.tensor_scalar(var[:], ssq[:], 1.0 / D, None, ALU.mult),
                        also=[("sa", sa_ss)])
                    sop(v, "sv", lambda: v.tensor_mul(tmp1[:], mu[:], mu[:]))
                    sop(v, "sv", lambda: v.tensor_sub(var[:], var[:], tmp1[:]))
                V(p1a_var)
                sv_var = C["sv"]

                def p1a_rstd(a, sv_var=sv_var):
                    sop(a, "sa", lambda: a.activation(rstd[:], var[:], AF.Ln,
                                                      bias=np.float32(EPS)),
                        also=[("sv", sv_var)])
                    sop(a, "sa", lambda: a.activation(rstd[:], rstd[:], AF.Exp,
                                                      scale=np.float32(-0.5)))
                A(p1a_rstd)
                sa_rstd = C["sa"]

                def p1a_aff(v, xb=xb, sa_rstd=sa_rstd):
                    sop(v, "sv", lambda: v.tensor_scalar(h_bf[:], xb[:], mu[:], rstd[:],
                                                         ALU.subtract, ALU.mult),
                        also=[("sa", sa_rstd)])
                V(p1a_aff)
                sv_affine_done.append(C["sv"])

                for c in range(16):
                    pt = ptrs[c % 2]

                    def p1a_tr(t, c=c, pt=pt, l=l):
                        t.wait_ge(SEM["sv"], sv_affine_done[l])
                        if l == 0 and c == 0:
                            t.wait_ge(SEM["pre"], pre_all)
                            t.wait_ge(SEM["preg"], preg_all)
                            t.wait_ge(SEM["sv"], sv_crow)
                        if tr_evac[c % 2] is not None:
                            t.wait_ge(SEM["sv"], tr_evac[c % 2])
                        t.transpose(pt[:, 0:128], h_bf[:, c * 128:(c + 1) * 128],
                                    id_s[:]).then_inc(SEM["pms"], 1)
                        C["pms"] += 1
                    T(p1a_tr)
                    pms_t = C["pms"]

                    def p1a_trev(v, c=c, pt=pt, l=l, pms_t=pms_t):
                        sop(v, "sv", lambda: v.tensor_copy(
                            arenaA[:, c * 2048 + l * 128: c * 2048 + (l + 1) * 128],
                            pt[:, 0:128]), also=[("pms", pms_t)])
                    V(p1a_trev)
                    tr_evac[c % 2] = C["sv"]
            sv_hT_done = C["sv"]

        p1b = contextlib.ExitStack()
        with p1b:
            bq_bf = p1b.enter_context(nc.sbuf_tensor("bq_bf", [1, FQKV], BF16))
            wsl0 = p1b.enter_context(nc.sbuf_tensor("wsl0", [128, 16 * 512], BF16))
            wsl1 = p1b.enter_context(nc.sbuf_tensor("wsl1", [128, 16 * 512], BF16))
            evb = [p1b.enter_context(nc.sbuf_tensor(f"ev{i}", [128, 512], BF16))
                   for i in range(4)]
            pqb = [p1b.enter_context(nc.psum_tensor(f"pq{i}", [128, 512], F32))
                   for i in range(4)]
            wslabs = [wsl0, wsl1]

            pm_qkv = {}
            sv_ev = {}
            qst_d = {}
            shv_tile_d = {}
            wld_after = {}
            order = [8, 9, 10, 11] + list(range(8))
            for fci, fc in enumerate(order):
                wb = wslabs[fci % 2]
                wsem = f"wld{fci % 2}"

                def p1b_wload(g, fc=fc, fci=fci, wb=wb, wsem=wsem):
                    if fci == 0:
                        g.wait_ge(SEM["sv"], sv_hT_done)
                        g.dma_start(out=bq_bf[:], in_=bqkv).then_inc(SEM["preg"], 16)
                        C["preg"] += 16
                    if fci >= 2:
                        g.wait_ge(SEM["pm"], pm_qkv[(order[fci - 2], NT - 1)])
                    srcw = WqkvT[:, fc * 512:(fc + 1) * 512].rearrange(
                        "(c p) f -> p c f", p=128)
                    dst = wb[:].rearrange("p (c f) -> p c f", c=16)
                    for q in range(4):
                        g.dma_start(out=dst[:, q * 4:(q + 1) * 4, :],
                                    in_=srcw[:, q * 4:(q + 1) * 4, :]
                                    ).then_inc(SEM[wsem], 16)
                        C[wsem] += 16
                G(p1b_wload)
                wld_after[fc] = (wsem, C[wsem])

                for tt in range(NT):
                    pq = pqb[tt % 4]
                    eb = evb[tt % 4]
                    key = (fc, tt - 4) if tt >= 4 else (
                        (order[fci - 1], tt + 12) if fci >= 1 else None)

                    def p1b_mm(t, fc=fc, fci=fci, tt=tt, pq=pq, key=key):
                        t.wait_ge(SEM[wld_after[fc][0]], wld_after[fc][1])
                        if tt == 0:
                            t.wait_ge(SEM["preg"], preg_all + 16)
                        if fci == 0 and tt == 0:
                            t.wait_ge(SEM["sv"], sv_hT_done)
                            t.wait_ge(SEM["pre"], pre_done)
                            t.wait_ge(SEM["preg"], preg_all)
                        if key in sv_ev:
                            t.wait_ge(SEM["sv"], sv_ev[key])
                        t.matmul(pq[:], ones_s[0:1, :],
                                 bq_bf[0:1, fc * 512:(fc + 1) * 512],
                                 start=True, stop=False)
                        for c in range(16):
                            mi = t.matmul(pq[:],
                                          arenaA[:, c * 2048 + tt * 128: c * 2048 + (tt + 1) * 128],
                                          wb[:, c * 512:(c + 1) * 512],
                                          start=False, stop=(c == 15))
                        mi.then_inc(SEM["pm"], 1)
                        C["pm"] += 1
                    T(p1b_mm)
                    pm_qkv[(fc, tt)] = C["pm"]

                    qsem = f"qst{tt % 4}"

                    def p1b_ev(v, fc=fc, tt=tt, pq=pq, eb=eb, key=key, qsem=qsem):
                        also = [("pm", pm_qkv[(fc, tt)])]
                        if key in qst_d:
                            also.append(qst_d[key])
                        sop(v, "sv", lambda: v.tensor_copy(eb[:], pq[:]), also=also)
                    V(p1b_ev)
                    sv_ev[(fc, tt)] = C["sv"]

                    if fc < 8:
                        def p1b_st(s, fc=fc, tt=tt, eb=eb, qsem=qsem):
                            s.wait_ge(SEM["sv"], sv_ev[(fc, tt)])
                            s.dma_start(out=qkv_raw[tt, :, fc * 512:(fc + 1) * 512],
                                        in_=eb[:]).then_inc(SEM[qsem], 16)
                            C[qsem] += 16
                        SY(p1b_st)
                        qst_d[(fc, tt)] = (qsem, C[qsem])
                    else:
                        def p1b_vst(sc, fc=fc, tt=tt, eb=eb):
                            sc.wait_ge(SEM["sv"], sv_ev[(fc, tt)])
                            if (fc, tt) != (8, 0):
                                sc.wait_ge(SEM["shv"], C["shv"])
                            c0 = 4 * (fc - 8)
                            sc.dma_start(
                                out=shardV[tt, c0:c0 + 4].rearrange("c w d -> w c d"),
                                in_=eb[:].rearrange("p (c d) -> p c d", c=4)
                            ).then_inc(SEM["shv"], 16)
                            C["shv"] += 16
                        A(p1b_vst)
                        qst_d[(fc, tt)] = ("shv", C["shv"])
                        shv_tile_d[tt] = C["shv"]
                        if fc == 11:
                            def p1b_ccv(g, tt=tt):
                                g.wait_ge(SEM["shv"], shv_tile_d[tt])
                                g.collective_compute(
                                    "AllGather", ALU.bypass, replica_groups=groups,
                                    ins=[shardV[tt].rearrange("c d w -> (c d) w")],
                                    outs=[gathV[tt].rearrange(
                                        "o c d w -> (o c d) w")],
                                ).then_inc(SEM[f"cv{tt}"])
                                C[f"cv{tt}"] += 1
                            G(p1b_ccv)
            qst_all = [(f"qst{i}", C[f"qst{i}"]) for i in range(4)]
            qst_all.append(("shv", C["shv"]))

        p1c = contextlib.ExitStack()
        with p1c:
            sin2 = p1c.enter_context(nc.sbuf_tensor("sin2", [128, 2 * D], F32))
            cos2 = p1c.enter_context(nc.sbuf_tensor("cos2", [128, 2 * D], F32))
            qk0 = p1c.enter_context(nc.sbuf_tensor("qk0", [128, 2 * D], BF16))
            qk1 = p1c.enter_context(nc.sbuf_tensor("qk1", [128, 2 * D], BF16))
            rot4 = p1c.enter_context(nc.sbuf_tensor("rot4", [128, 2 * D], F32))
            rt14 = p1c.enter_context(nc.sbuf_tensor("rt14", [128, 2 * D], F32))
            ssqk = p1c.enter_context(nc.sbuf_tensor("ssqk", [128, 1], F32))
            rstdk = p1c.enter_context(nc.sbuf_tensor("rstdk", [128, 1], F32))
            qf_bf = p1c.enter_context(nc.sbuf_tensor("qf_bf", [128, D], BF16))
            kts = [p1c.enter_context(nc.sbuf_tensor(f"kt{i}", [128, D], BF16))
                   for i in range(2)]
            ptr2 = p1c.enter_context(nc.psum_tensor("ptr2", [128, 1024], BF16))
            ptr3 = p1c.enter_context(nc.psum_tensor("ptr3", [128, 1024], BF16))
            qkbufs = [qk0, qk1]
            ptrs2 = [ptr2, ptr3]

            def p1c_consts(s):
                for nm, val in qst_all:
                    s.wait_ge(SEM[nm], val)
                s.dma_start(out=sin2[:], in_=sin2d).then_inc(SEM["scl"], 16)
                s.dma_start(out=cos2[:], in_=cos2d).then_inc(SEM["scl"], 16)
                C["scl"] += 32
            SY(p1c_consts)
            scl_all = C["scl"]

            sv_p1c_done = []
            shst_k = []
            sv_tile_q = []
            tr2_evac = [None, None]
            for l in range(NT):
                qkb = qkbufs[l % 2]
                ktb = kts[l % 2]
                lsem = f"qld{l % 2}"

                def p1c_load(s, l=l, qkb=qkb, lsem=lsem):
                    if l >= 2:
                        s.wait_ge(SEM["sv"], sv_p1c_done[l - 2][0])
                        s.wait_ge(SEM["sg"], sv_p1c_done[l - 2][1])
                    s.dma_start(out=qkb[:], in_=qkv_raw[l, :, 0:2 * D]
                                ).then_inc(SEM[lsem], 16)
                    C[lsem] += 16
                SY(p1c_load)
                qld_t = (lsem, C[lsem])

                def p1c_ropeq(v, qkb=qkb, qld_t=qld_t):
                    qv = qkb[:, 0:D].rearrange("p (f two) -> p f two", two=2)
                    evens = qv[:, :, 0:1].squeeze(2)
                    odds = qv[:, :, 1:2].squeeze(2)
                    sop(v, "sv", lambda: v.tensor_scalar(
                        rot4[:, 0:1024], odds, -1.0, None, ALU.mult),
                        also=[qld_t, ("scl", scl_all)])
                    sop(v, "sv", lambda: v.tensor_copy(rot4[:, 1024:2048], evens))
                    sop(v, "sv", lambda: v.tensor_mul(rt14[:, 0:D], qkb[:, 0:D],
                                                      cos2[:, 0:D]))
                    sop(v, "sv", lambda: v.tensor_mul(rot4[:, 0:D], rot4[:, 0:D],
                                                      sin2[:, 0:D]))
                    sop(v, "sv", lambda: v.tensor_add(rot4[:, 0:D], rot4[:, 0:D],
                                                      rt14[:, 0:D]))
                V(p1c_ropeq)
                sv_rope = C["sv"]

                def p1c_ropek(g, qkb=qkb, qld_t=qld_t):
                    kv2 = qkb[:, D:2 * D].rearrange("p (f two) -> p f two", two=2)
                    evens = kv2[:, :, 0:1].squeeze(2)
                    odds = kv2[:, :, 1:2].squeeze(2)
                    sop(g, "sg", lambda: g.tensor_scalar(
                        rot4[:, D:D + 1024], odds, -1.0, None, ALU.mult),
                        also=[qld_t, ("scl", scl_all)])
                    sop(g, "sg", lambda: g.tensor_copy(rot4[:, D + 1024:2 * D], evens))
                    sop(g, "sg", lambda: g.tensor_mul(rt14[:, D:2 * D], qkb[:, D:2 * D],
                                                      cos2[:, D:2 * D]))
                    sop(g, "sg", lambda: g.tensor_mul(rot4[:, D:2 * D], rot4[:, D:2 * D],
                                                      sin2[:, D:2 * D]))
                    sop(g, "sg", lambda: g.tensor_add(rot4[:, D:2 * D], rot4[:, D:2 * D],
                                                      rt14[:, D:2 * D]))
                G(p1c_ropek)
                sg_rope = C["sg"]

                def p1c_acts(a, qkb=qkb, l=l, qld_t=qld_t, sv_rope=sv_rope,
                             sg_rope=sg_rope):
                    sop(a, "sa", lambda: a.activation(sq_dump[:], rot4[:, 0:D],
                                                      AF.Square, accum_out=ssq[:]),
                        also=[("sv", sv_rope)])
                    sop(a, "sa", lambda: a.activation(rstd[:], ssq[:], AF.Ln,
                                                      bias=np.float32(1e-24)))
                    sop(a, "sa", lambda: a.activation(
                        rstd[:], rstd[:], AF.Exp, scale=np.float32(-0.5),
                        bias=np.float32(np.log(SCALE))))
                    sop(a, "sa", lambda: a.activation(sq_dump[:], rot4[:, D:2 * D],
                                                      AF.Square, accum_out=ssqk[:]),
                        also=[("sg", sg_rope)])
                    sop(a, "sa", lambda: a.activation(rstdk[:], ssqk[:], AF.Ln,
                                                      bias=np.float32(1e-24)))
                    sop(a, "sa", lambda: a.activation(
                        rstdk[:], rstdk[:], AF.Exp, scale=np.float32(-0.5),
                        bias=np.float32(np.log(SCALE))))
                A(p1c_acts)
                sa_rnq = C["sa"] - 3
                sa_rnk = C["sa"]

                def p1c_scaleq(v, sa_rnq=sa_rnq):
                    sop(v, "sv", lambda: v.tensor_scalar(qf_bf[:], rot4[:, 0:D],
                                                         rstd[:], None, ALU.mult),
                        also=[("sa", sa_rnq)])
                V(p1c_scaleq)
                sv_kf = C["sv"]

                def p1c_scalek(g, ktb=ktb, l=l, sa_rnk=sa_rnk):
                    also_k = [("sa", sa_rnk)]
                    if l >= 2:
                        also_k.append(("shk", shst_k[l - 2]))
                    sop(g, "sg", lambda: g.tensor_scalar(ktb[:], rot4[:, D:2 * D],
                                                         rstdk[:], None, ALU.mult),
                        also=also_k)
                G(p1c_scalek)
                sg_kf = C["sg"]
                sv_p1c_done.append((C["sv"], C["sg"]))

                for c in range(16):
                    pt = ptrs2[c % 2]

                    def p1c_trq(t, c=c, pt=pt, l=l, sv_kf=sv_kf):
                        t.wait_ge(SEM["sv"], sv_kf)
                        if tr2_evac[c % 2] is not None:
                            t.wait_ge(SEM[tr2_evac[c % 2][0]], tr2_evac[c % 2][1])
                        t.transpose(pt[:, 0:128], qf_bf[:, c * 128:(c + 1) * 128],
                                    id_s[:]).then_inc(SEM["pms"], 1)
                        C["pms"] += 1
                    T(p1c_trq)
                    pms_q = C["pms"]

                    def p1c_trqev(v, c=c, pt=pt, l=l, pms_q=pms_q):
                        sop(v, "sv", lambda: v.tensor_copy(
                            arenaA[:, c * 2048 + l * 128: c * 2048 + (l + 1) * 128],
                            pt[:, 0:128]), also=[("pms", pms_q)])
                    V(p1c_trqev)
                    tr2_evac[c % 2] = ("sv", C["sv"])
                sv_tile_q.append(C["sv"])

                for c in range(16):
                    pt = ptrs2[c % 2]

                    def p1c_trk(t, c=c, pt=pt, ktb=ktb, sg_kf=sg_kf):
                        t.wait_ge(SEM["sg"], sg_kf)
                        t.wait_ge(SEM[tr2_evac[c % 2][0]], tr2_evac[c % 2][1])
                        t.transpose(pt[:, 0:128], ktb[:, c * 128:(c + 1) * 128],
                                    id_s[:]).then_inc(SEM["pms"], 1)
                        C["pms"] += 1
                    T(p1c_trk)
                    pms_k = C["pms"]

                    def p1c_trkev(a, c=c, pt=pt, ktb=ktb, pms_k=pms_k):
                        sop(a, "sa", lambda: a.activation(
                            ktb[:, c * 128:(c + 1) * 128], pt[:, 0:128],
                            AF.Identity), also=[("pms", pms_k)])
                    A(p1c_trkev)
                    tr2_evac[c % 2] = ("sa", C["sa"])
                sa_kt = C["sa"]

                def p1c_shk(sc, l=l, ktb=ktb, sa_kt=sa_kt):
                    sc.wait_ge(SEM["sa"], sa_kt)
                    if l >= 1:
                        sc.wait_ge(SEM["shk"], shst_k[l - 1])
                    sc.dma_start(out=shardK[l].rearrange("c d w -> d c w"),
                                 in_=ktb[:].rearrange("p (c w) -> p c w", c=16)
                                 ).then_inc(SEM["shk"], 16)
                    C["shk"] += 16
                A(p1c_shk)
                shst_k.append(C["shk"])

                def p1c_cc(g, l=l):
                    g.wait_ge(SEM["shk"], shst_k[l])
                    g.collective_compute(
                        "AllGather", ALU.bypass, replica_groups=groups,
                        ins=[shardK[l].rearrange("c d w -> (c d) w")],
                        outs=[gathK[l].rearrange("o c d w -> (o c d) w")],
                    ).then_inc(SEM[f"cc{l}"])
                    C[f"cc{l}"] += 1
                G(p1c_cc)
            sv_p1c_end = C["sv"]
            sg_p1c_end = C["sg"]
            shst_all = (C["shk"], C["shv"])

        p23 = contextlib.ExitStack()
        arenaB = p23.enter_context(nc.sbuf_tensor("arenaB", [128, 32768], BF16))
        with p23:
            p2 = contextlib.ExitStack()
            with p2:
                mask_s = p2.enter_context(nc.sbuf_tensor("mask_s", [128, 512], BF16))
                kbuf = p2.enter_context(nc.sbuf_tensor("kbuf", [128, 8 * 512], BF16))
                vbuf = p2.enter_context(nc.sbuf_tensor("vbuf", [128, 8 * 512], BF16))
                PT = p2.enter_context(nc.sbuf_tensor("PT", [128, 3072], BF16))
                rdn = p2.enter_context(nc.sbuf_tensor("rdn", [128, 512], F32))
                ST = p2.enter_context(nc.psum_tensor("ST", [128, 3072], F32))
                AT = p2.enter_context(nc.psum_tensor("AT", [128, 512], F32))
                DN = p2.enter_context(nc.psum_tensor("DN", [128, 512], F32))

                kv_sched = []
                for h in range(H):
                    for p in range(4):
                        for l in range(4 * p + 4):
                            kv_sched.append((h, p, l))

                pm_visit = []
                exp_sa = {}
                pmv_pair = {}
                mask_cnt = {}
                sv_p2_0 = C["sv"]
                sa_p2_0 = C["sa"]
                atden_free = (sv_p2_0, sa_p2_0)

                def emit_kv_dma(s_idx):
                    hh, pp, ll = kv_sched[s_idx]
                    par = s_idx % 8

                    def f(sy, hh=hh, ll=ll, par=par, s_idx=s_idx):
                        if s_idx < 8:
                            sy.wait_ge(SEM["sv"], sv_p1c_end)
                            sy.wait_ge(SEM["sg"], sg_p1c_end)
                            sy.wait_ge(SEM["shk"], shst_all[0])
                            sy.wait_ge(SEM["shv"], shst_all[1])
                        sy.wait_ge(SEM[f"cc{ll}"], 1)
                        sy.wait_ge(SEM[f"cv{ll}"], 1)
                        if s_idx >= 8:
                            sy.wait_ge(SEM["pmv"], pm_visit[s_idx - 8])
                        sy.dma_start(
                            out=kbuf[:, par * 512:(par + 1) * 512]
                            .rearrange("p (o w) -> p o w", o=4),
                            in_=gathK[ll, :, hh].rearrange("o d w -> d o w")
                        ).then_inc(SEM[f"kld{par}"], 16)
                        C[f"kld{par}"] += 16
                        sy.dma_start(
                            out=vbuf[:, par * 512:(par + 1) * 512]
                            .rearrange("p (o d) -> p o d", o=4),
                            in_=gathV[ll, :, hh].rearrange("o w d -> w o d")
                        ).then_inc(SEM[f"vld{par}"], 16)
                        C[f"vld{par}"] += 16
                    SY(f)

                def p2_mload(s):
                    s.wait_ge(SEM["sv"], sv_p1c_end)
                    s.dma_start(out=mask_s[:].rearrange("p (o w) -> p o w", o=4),
                                in_=masksd.rearrange("o p w -> p o w")
                                ).then_inc(SEM["scl"], 16)
                    C["scl"] += 16
                SY(p2_mload)
                scl_mask = C["scl"]

                for _s in range(min(7, len(kv_sched))):
                    emit_kv_dma(_s)

                visit = 0
                pass_idx = 0

                def sched_params(s_idx):
                    hh, pp, ll = kv_sched[s_idx]
                    i_lo = 128 * (ll - 4 * pp) if ll >= 4 * pp else 0
                    return (hh, pp, ll, i_lo, hh * 2048 + 512 * pp)

                def emit_s_block(s_idx):
                    hh, pp, ll, i_lo, qbase = sched_params(s_idx)
                    par = s_idx % 8
                    kv_need = 16 * (s_idx // 8 + 1)
                    pairs = (2 * s_idx, 2 * s_idx + 1)

                    def p2_s(t, par=par, i_lo=i_lo, qbase=qbase,
                             kv_need=kv_need, s_idx=s_idx, pairs=pairs):
                        t.wait_ge(SEM[f"kld{par}"], kv_need)
                        if s_idx == 0:
                            t.wait_ge(SEM["sv"], sv_p1c_end)
                            t.wait_ge(SEM["sg"], sg_p1c_end)
                        for cp, pi in enumerate(pairs):
                            slot = pi % 3
                            if pi - 3 in exp_sa:
                                t.wait_ge(SEM["sa"], exp_sa[pi - 3])
                            for jb2 in (0, 1):
                                jb = 2 * cp + jb2
                                mi = t.matmul(
                                    ST[:, slot * 1024 + jb2 * 512 + i_lo:
                                       slot * 1024 + (jb2 + 1) * 512],
                                    kbuf[:, par * 512 + jb * 128:par * 512 + (jb + 1) * 128],
                                    arenaA[:, qbase + i_lo:qbase + 512],
                                    start=True, stop=True)
                            mi.then_inc(SEM["pms"], 1)
                            C["pms"] += 1
                    T(p2_s)
                    pms_sblock[s_idx] = (C["pms"] - 1, C["pms"])

                pms_sblock = {}
                emit_s_block(0)

                for h in range(H):
                    for p in range(4):
                        L = 4 * p + 4
                        for l in range(L):
                            s_idx = visit
                            par = s_idx % 8
                            i_lo = 128 * (l - 4 * p) if l >= 4 * p else 0
                            diag = l >= 4 * p
                            kv_need = 16 * (s_idx // 8 + 1)
                            first_pv = (l == 0)
                            pairs = (2 * s_idx, 2 * s_idx + 1)
                            pms_pairs = pms_sblock[s_idx]

                            if s_idx + 1 < len(kv_sched):
                                emit_s_block(s_idx + 1)

                            for cp, pi in enumerate(pairs):
                                slot = pi % 3

                                def p2_exp(a, cp=cp, pi=pi, slot=slot, i_lo=i_lo,
                                           pp2=pms_pairs[cp]):
                                    a.wait_ge(SEM["pms"], pp2)
                                    if pi - 3 in pmv_pair:
                                        a.wait_ge(SEM["pmv"], pmv_pair[pi - 3])
                                    sop(a, "sa", lambda: a.activation(
                                        PT[:, slot * 1024 + i_lo:(slot + 1) * 1024],
                                        ST[:, slot * 1024 + i_lo:(slot + 1) * 1024],
                                        AF.Exp))
                                A(p2_exp)
                                exp_sa[pi] = C["sa"]

                                if diag:
                                    for jb2 in (0, 1):
                                        def p2_mask(g, jb2=jb2, cp=cp, slot=slot,
                                                    i_lo=i_lo, pi=pi):
                                            jb = 2 * cp + jb2
                                            off = slot * 1024 + jb2 * 512 + i_lo
                                            sop(g, "sg", lambda: g.tensor_mul(
                                                PT[:, off:off + 128],
                                                PT[:, off:off + 128],
                                                mask_s[:, jb * 128:(jb + 1) * 128]),
                                                also=[("sa", exp_sa[pi]),
                                                      ("scl", scl_mask)])
                                        G(p2_mask)
                                        mask_cnt[(s_idx, 2 * cp + jb2)] = C["sg"]

                            def p2_pv(t, s_idx=s_idx, par=par, i_lo=i_lo, diag=diag,
                                      first_pv=first_pv, l=l, L=L, kv_need=kv_need,
                                      pairs=pairs):
                                t.wait_ge(SEM[f"vld{par}"], kv_need)
                                if first_pv:
                                    t.wait_ge(SEM["sv"], atden_free[0])
                                    t.wait_ge(SEM["sa"], atden_free[1])
                                if diag:
                                    t.wait_ge(SEM["sg"], mask_cnt[(s_idx, 3)])
                                for cp, pi in enumerate(pairs):
                                    slot = pi % 3
                                    t.wait_ge(SEM["sa"], exp_sa[pi])
                                    for jb2 in (0, 1):
                                        jb = 2 * cp + jb2
                                        st_mm = (l == 0 and jb == 0)
                                        sp_mm = (l == L - 1 and jb == 3)
                                        pt_sl = PT[:, slot * 1024 + jb2 * 512 + i_lo:
                                                   slot * 1024 + (jb2 + 1) * 512]
                                        t.matmul(AT[:, i_lo:512],
                                                 vbuf[:, par * 512 + jb * 128:par * 512 + (jb + 1) * 128],
                                                 pt_sl, start=st_mm, stop=sp_mm)
                                        mi = t.matmul(DN[:, i_lo:512], ones_s[:],
                                                      pt_sl, start=st_mm, stop=sp_mm)
                                    mi.then_inc(SEM["pmv"], 1)
                                    C["pmv"] += 1
                                    pmv_pair[pi] = C["pmv"]
                            T(p2_pv)
                            pm_visit.append(C["pmv"])

                            visit += 1
                            if visit + 6 < len(kv_sched):
                                emit_kv_dma(visit + 6)

                        pm_pass = pm_visit[-1]

                        def p2_rden(a, pm_pass=pm_pass, sv_ev=atden_free[0]):
                            sop(a, "sa", lambda: a.activation(rdn[:], DN[:], AF.Ln),
                                also=[("pmv", pm_pass), ("sv", sv_ev)])
                            sop(a, "sa", lambda: a.activation(rdn[:], rdn[:], AF.Exp,
                                                              scale=np.float32(-1.0)))
                        A(p2_rden)
                        sa_rden = C["sa"]

                        def p2_evac(v, h=h, p=p, pm_pass=pm_pass, sa_rden=sa_rden):
                            sop(v, "sv", lambda: v.tensor_mul(
                                arenaB[:, h * 2048 + 512 * p: h * 2048 + 512 * (p + 1)],
                                AT[:], rdn[:]),
                                also=[("pmv", pm_pass), ("sa", sa_rden)])
                        V(p2_evac)
                        atden_free = (C["sv"], C["sa"])
                        pass_idx += 1

                    def p2_wout(g, h=h):
                        g.wait_ge(SEM["pmv"], pm_visit[-1])
                        g.dma_start(
                            out=arenaA[:, h * 2048:(h + 1) * 2048].unsqueeze(1),
                            in_=WoutTd.rearrange("(c p) f -> p c f", p=128)[:, h:h + 1, :]
                        ).then_inc(SEM["w2ld"], 16)
                        C["w2ld"] += 16
                    G(p2_wout)
                sv_at_done = C["sv"]
                sa_p2_end = C["sa"]
                pm_p2_end = C["pmv"]
                w2_all = C["w2ld"]

            p3 = contextlib.ExitStack()
            with p3:
                xs = p3.enter_context(nc.sbuf_tensor("p3xs", [128, D], F32))
                ao = p3.enter_context(nc.sbuf_tensor("p3ao", [128, D], F32))
                x2 = p3.enter_context(nc.sbuf_tensor("p3x2", [128, D], F32))
                h2f = p3.enter_context(nc.sbuf_tensor("p3h2f", [128, D], F32))
                h2b = p3.enter_context(nc.sbuf_tensor("p3h2b", [128, D], BF16))
                h2T = p3.enter_context(nc.sbuf_tensor("p3h2T", [128, D], BF16))
                otb = p3.enter_context(nc.sbuf_tensor("p3ot", [128, D], F32))
                Dc = p3.enter_context(nc.sbuf_tensor("p3Dc", [128, 128], F32))
                D2t = p3.enter_context(nc.sbuf_tensor("p3D2", [128, 64], F32))
                sco = p3.enter_context(nc.sbuf_tensor("p3sco", [128, 8], F32))
                nd = p3.enter_context(nc.sbuf_tensor("p3nd", [128, 8], F32))
                zt = p3.enter_context(nc.sbuf_tensor("p3zt", [128, 8], F32))
                gt = p3.enter_context(nc.sbuf_tensor("p3gt", [128, 8], F32))
                mx = p3.enter_context(nc.sbuf_tensor("p3mx", [128, 1], F32))
                eq = p3.enter_context(nc.sbuf_tensor("p3eq", [128, 8], F32))
                t8a = p3.enter_context(nc.sbuf_tensor("p3t8a", [128, 8], F32))
                t8b = p3.enter_context(nc.sbuf_tensor("p3t8b", [128, 8], F32))
                oh = p3.enter_context(nc.sbuf_tensor("p3oh", [128, 8], F32))
                M2 = p3.enter_context(nc.sbuf_tensor("p3M2", [128, 72], BF16))
                M2T = p3.enter_context(nc.sbuf_tensor("p3M2T", [72, 128], BF16))
                ssq4 = p3.enter_context(nc.sbuf_tensor("p3ssq4", [128, 4], F32))
                s14 = p3.enter_context(nc.sbuf_tensor("p3s14", [128, 4], F32))
                pob = [p3.enter_context(nc.psum_tensor(f"po{i}", [128, 512], F32))
                       for i in range(4)]
                pf = p3.enter_context(nc.psum_tensor("p3pf", [128, 512], F32))
                ptr4 = p3.enter_context(nc.psum_tensor("ptr4", [128, 1024], BF16))
                ptr5 = p3.enter_context(nc.psum_tensor("ptr5", [128, 1024], BF16))
                ffn = xs
                ot = otb
                thg = sq_dump

                p3_done = []
                po_evac = {}
                pf_evac = {}
                ffn_evac = {}
                pm_op_d = {}
                p3tr_ev = [None, None]

                def ln_combine(sa_chunks, extra_ss=False):
                    def fv2(v, sa_chunks=sa_chunks):
                        sop(v, "sv", lambda: v.reduce_sum(s1[:], s14[:], axis=AX.X),
                            also=[("sa", sa_chunks)])
                        sop(v, "sv", lambda: v.reduce_sum(ssq[:], ssq4[:], axis=AX.X))
                        sop(v, "sv", lambda: v.tensor_scalar(mu[:], s1[:], 1.0 / D,
                                                             None, ALU.mult))
                        sop(v, "sv", lambda: v.tensor_scalar(var[:], ssq[:], 1.0 / D,
                                                             None, ALU.mult))
                        sop(v, "sv", lambda: v.tensor_mul(tmp1[:], mu[:], mu[:]))
                        sop(v, "sv", lambda: v.tensor_sub(var[:], var[:], tmp1[:]))
                        if extra_ss:
                            sop(v, "sv", lambda: v.tensor_scalar(tmp1[:], var[:], EPS,
                                                                 None, ALU.add))
                            sop(v, "sv", lambda: v.reciprocal(tmp1[:], tmp1[:]))
                            sop(v, "sv", lambda: v.tensor_scalar(ssh2[:], var[:],
                                                                 float(D), None, ALU.mult))
                            sop(v, "sv", lambda: v.tensor_mul(ssh2[:], ssh2[:], tmp1[:]))
                    V(fv2)
                    sv_v = C["sv"]

                    def fa2(a, sv_v=sv_v):
                        sop(a, "sa", lambda: a.activation(rstd[:], var[:], AF.Ln,
                                                          bias=np.float32(EPS)),
                            also=[("sv", sv_v)])
                        sop(a, "sa", lambda: a.activation(rstd[:], rstd[:], AF.Exp,
                                                          scale=np.float32(-0.5)))
                    A(fa2)
                    return C["sa"]

                def ln_stats(src, extra_ss=False):
                    sv_src = C["sv"]

                    def fa(a, sv_src=sv_src):
                        sop(a, "sa", lambda: a.activation(sq_dump[:], src[:], AF.Square,
                                                          accum_out=ssq[:]),
                            also=[("sv", sv_src)])
                        sop(a, "sa", lambda: a.activation(sq_dump[:], src[:],
                                                          AF.Identity,
                                                          accum_out=s1[:]))
                    A(fa)
                    sa_s = C["sa"]

                    def fv2(v, sa_s=sa_s):
                        sop(v, "sv", lambda: v.tensor_scalar(mu[:], s1[:], 1.0 / D,
                                                             None, ALU.mult),
                            also=[("sa", sa_s)])
                        sop(v, "sv", lambda: v.tensor_scalar(var[:], ssq[:], 1.0 / D,
                                                             None, ALU.mult))
                        sop(v, "sv", lambda: v.tensor_mul(tmp1[:], mu[:], mu[:]))
                        sop(v, "sv", lambda: v.tensor_sub(var[:], var[:], tmp1[:]))
                        if extra_ss:
                            sop(v, "sv", lambda: v.tensor_scalar(tmp1[:], var[:], EPS,
                                                                 None, ALU.add))
                            sop(v, "sv", lambda: v.reciprocal(tmp1[:], tmp1[:]))
                            sop(v, "sv", lambda: v.tensor_scalar(ssh2[:], var[:],
                                                                 float(D), None, ALU.mult))
                            sop(v, "sv", lambda: v.tensor_mul(ssh2[:], ssh2[:], tmp1[:]))
                    V(fv2)
                    sv_v = C["sv"]

                    def fa2(a, sv_v=sv_v):
                        sop(a, "sa", lambda: a.activation(rstd[:], var[:], AF.Ln,
                                                          bias=np.float32(EPS)),
                            also=[("sv", sv_v)])
                        sop(a, "sa", lambda: a.activation(rstd[:], rstd[:], AF.Exp,
                                                          scale=np.float32(-0.5)))
                    A(fa2)
                    return C["sa"]

                def emit_op(u):
                    def p3_op(t, u=u):
                        if u == 0:
                            t.wait_ge(SEM["sv"], sv_at_done)
                            t.wait_ge(SEM["sa"], sa_p2_end)
                            t.wait_ge(SEM["w2ld"], w2_all)
                            t.wait_ge(SEM["preg"], preg_all)
                        for fc in range(4):
                            if (u - 1, fc) in po_evac:
                                t.wait_ge(SEM["sv"], po_evac[(u - 1, fc)])
                            t.matmul(pob[fc][:], ones_s[0:1, :],
                                     bo_bf[0:1, fc * 512:(fc + 1) * 512],
                                     start=True, stop=False)
                            for c in range(16):
                                mi = t.matmul(pob[fc][:],
                                              arenaB[:, c * 2048 + u * 128: c * 2048 + (u + 1) * 128],
                                              arenaA[:, c * 2048 + fc * 512: c * 2048 + (fc + 1) * 512],
                                              start=False, stop=(c == 15))
                        mi.then_inc(SEM["pm"], 1)
                        C["pm"] += 1
                    T(p3_op)
                    pm_op_d[u] = C["pm"]

                for u in range(NT):
                    emit_op(u)
                    xsem = f"xld{u % 2}"

                    def p3_xload(s, u=u, xsem=xsem):
                        if u == 0:
                            s.wait_ge(SEM["pmv"], pm_p2_end)
                        if u >= 1:
                            s.wait_ge(SEM["sv"], p3_done[u - 1][0])
                            s.wait_ge(SEM["sg"], p3_done[u - 1][1])
                        s.dma_start(out=xs[:], in_=x_sh[u * 128:(u + 1) * 128, :]
                                    ).then_inc(SEM[xsem], 16)
                        C[xsem] += 16
                    SY(p3_xload)
                    xld_u = (xsem, C[xsem])

                    for fc in range(4):
                        def p3_aoev(v, u=u, fc=fc):
                            also = [("pm", pm_op_d[u])] if fc == 0 else []
                            sop(v, "sv", lambda: v.tensor_copy(
                                ao[:, fc * 512:(fc + 1) * 512], pob[fc][:]), also=also)
                            po_evac[(u, fc)] = C["sv"]
                        V(p3_aoev)

                        def p3_ach(a, u=u, fc=fc):
                            sop(a, "sa", lambda: a.activation(
                                h2b[:, fc * 512:(fc + 1) * 512],
                                ao[:, fc * 512:(fc + 1) * 512], AF.Square,
                                accum_out=ssq4[:, fc:fc + 1]),
                                also=[("sv", po_evac[(u, fc)])])
                            sop(a, "sa", lambda: a.activation(
                                h2b[:, fc * 512:(fc + 1) * 512],
                                ao[:, fc * 512:(fc + 1) * 512], AF.Identity,
                                accum_out=s14[:, fc:fc + 1]))
                        A(p3_ach)
                    sa_ln1 = ln_combine(C["sa"])

                    def p3_x2(v, xld_u=xld_u, sa_ln1=sa_ln1):
                        sop(v, "sv", lambda: v.tensor_scalar(thg[:], ao[:], mu[:],
                                                             rstd[:], ALU.subtract,
                                                             ALU.mult),
                            also=[("sa", sa_ln1), xld_u])
                    V(p3_x2)
                    sv_thg1 = C["sv"]

                    def p3_x2g(g, sv_thg1=sv_thg1):
                        sop(g, "sg", lambda: g.tensor_add(x2[:], xs[:], thg[:]),
                            also=[("sv", sv_thg1)])
                    G(p3_x2g)
                    sg_x2 = C["sg"]

                    def p3_x2sync(a, sg_x2=sg_x2):
                        a.wait_ge(SEM["sg"], sg_x2)
                    A(p3_x2sync)

                    def p3_x2syncv(v, sg_x2=sg_x2):
                        v.wait_ge(SEM["sg"], sg_x2)
                    V(p3_x2syncv)
                    sa_ln2 = ln_stats(x2, extra_ss=True)

                    def p3_h2(v, sa_ln2=sa_ln2):
                        sop(v, "sv", lambda: v.tensor_scalar(h2b[:], x2[:], mu[:],
                                                             rstd[:], ALU.subtract,
                                                             ALU.mult),
                            also=[("sa", sa_ln2)])
                    V(p3_h2)
                    sv_h2b = C["sv"]

                    def p3_h2f(v):
                        sop(v, "sv", lambda: v.tensor_scalar(h2f[:], x2[:], mu[:],
                                                             rstd[:], ALU.subtract,
                                                             ALU.mult))
                    V(p3_h2f)

                    for c in range(16):
                        pt = [ptr4, ptr5][c % 2]

                        def p3_tr(t, c=c, pt=pt, sv_h2b=sv_h2b, u=u):
                            t.wait_ge(SEM["sv"], sv_h2b)
                            if p3tr_ev[c % 2] is not None:
                                t.wait_ge(SEM[p3tr_ev[c % 2][0]], p3tr_ev[c % 2][1])
                            t.transpose(pt[:, 0:128], h2b[:, c * 128:(c + 1) * 128],
                                        id_s[:]).then_inc(SEM["pms"], 1)
                            C["pms"] += 1
                        T(p3_tr)
                        pms_t = C["pms"]

                        if c % 2 == 0:
                            def p3_trev(v, c=c, pt=pt, pms_t=pms_t):
                                sop(v, "sv", lambda: v.tensor_copy(
                                    h2T[:, c * 128:(c + 1) * 128], pt[:, 0:128]),
                                    also=[("pms", pms_t)])
                            V(p3_trev)
                            p3tr_ev[c % 2] = ("sv", C["sv"])
                        else:
                            def p3_trev(a, c=c, pt=pt, pms_t=pms_t):
                                sop(a, "sa", lambda: a.activation(
                                    h2T[:, c * 128:(c + 1) * 128], pt[:, 0:128],
                                    AF.Identity), also=[("pms", pms_t)])
                            A(p3_trev)
                            p3tr_ev[c % 2] = ("sa", C["sa"])
                    sv_h2T = C["sv"]
                    sa_h2T = C["sa"]

                    def p3_mm1(t, u=u, sv_h2T=sv_h2T, sa_h2T=sa_h2T):
                        t.wait_ge(SEM["sv"], sv_h2T)
                        t.wait_ge(SEM["sa"], sa_h2T)
                        if (u - 1) in pf_evac:
                            t.wait_ge(SEM["sv"], pf_evac[u - 1])
                        for c in range(16):
                            mi = t.matmul(pf[:, 0:136],
                                          h2T[:, c * 128:(c + 1) * 128],
                                          wcat_s[:, c * 144:c * 144 + 136],
                                          start=(c == 0), stop=(c == 15))
                        mi.then_inc(SEM["pm"], 1)
                        C["pm"] += 1
                    T(p3_mm1)
                    pm_mm1 = C["pm"]

                    def p3_gate1(v, u=u, pm_mm1=pm_mm1):
                        sop(v, "sv", lambda: v.tensor_sub(Dc[:], pf[:, 0:128], ccu_b[:]),
                            also=[("pm", pm_mm1)])
                        sop(v, "sv", lambda: v.tensor_mul(D2t[:], Dc[:, 0:64], Dc[:, 0:64]))
                        sop(v, "sv", lambda: v.reduce_sum(
                            sco[:], D2t[:].rearrange("p (n s) -> p n s", n=8), axis=AX.X))
                        sop(v, "sv", lambda: v.tensor_scalar(nd[:], pf[:, 128:136],
                                                             -2.0, ssh2[:], ALU.mult,
                                                             ALU.add))
                        pf_evac[u] = C["sv"]
                        sop(v, "sv", lambda: v.tensor_add(nd[:], nd[:], mm_b[:]))
                        sop(v, "sv", lambda: v.reciprocal(nd[:], nd[:]))
                        sop(v, "sv", lambda: v.tensor_mul(sco[:], sco[:], nd[:]))
                        sop(v, "sv", lambda: v.tensor_sub(zt[:], sco[:], center_s[:]))
                        sop(v, "sv", lambda: v.tensor_mul(zt[:], zt[:], slope_s[:]))
                        sop(v, "sv", lambda: v.tensor_scalar(zt[:], zt[:], -80.0,
                                                             None, ALU.max))
                    V(p3_gate1)
                    sv_z = C["sv"]

                    def p3_gate2(a, sv_z=sv_z):
                        sop(a, "sa", lambda: a.activation(gt[:], zt[:], AF.Exp,
                                                          scale=np.float32(-1.0)),
                            also=[("sv", sv_z)])
                    A(p3_gate2)
                    sa_g = C["sa"]

                    def p3_gate3(v, sa_g=sa_g):
                        sop(v, "sv", lambda: v.tensor_scalar(gt[:], gt[:], 1.0, None,
                                                             ALU.add),
                            also=[("sa", sa_g)])
                        sop(v, "sv", lambda: v.reciprocal(gt[:], gt[:]))
                        sop(v, "sv", lambda: v.reduce_max(mx[:], gt[:], axis=AX.X))
                        sop(v, "sv", lambda: v.tensor_scalar(eq[:], gt[:], mx[:],
                                                             None, ALU.is_ge))
                        sop(v, "sv", lambda: v.tensor_copy(t8a[:, 0:1], eq[:, 0:1]))
                        sop(v, "sv", lambda: v.tensor_add(t8a[:, 1:8], eq[:, 1:8],
                                                          eq[:, 0:7]))
                        sop(v, "sv", lambda: v.tensor_copy(t8b[:, 0:2], t8a[:, 0:2]))
                        sop(v, "sv", lambda: v.tensor_add(t8b[:, 2:8], t8a[:, 2:8],
                                                          t8a[:, 0:6]))
                        sop(v, "sv", lambda: v.tensor_copy(t8a[:, 0:4], t8b[:, 0:4]))
                        sop(v, "sv", lambda: v.tensor_add(t8a[:, 4:8], t8b[:, 4:8],
                                                          t8b[:, 0:4]))
                        sop(v, "sv", lambda: v.tensor_scalar(oh[:], t8a[:], 1.5,
                                                             None, ALU.is_le))
                        sop(v, "sv", lambda: v.tensor_mul(oh[:], oh[:], eq[:]))
                        sop(v, "sv", lambda: v.tensor_mul(
                            M2[:, 0:64].rearrange("p (n k) -> p n k", n=8),
                            Dc[:, 64:128].rearrange("p (n k) -> p n k", n=8),
                            oh[:].unsqueeze(2).broadcast_to([128, 8, 8])))
                        sop(v, "sv", lambda: v.tensor_copy(M2[:, 64:72], oh[:]))
                        sop(v, "sv", lambda: v.scalar_tensor_tensor(
                            thg[:], h2f[:], mx[:], h2f[:], ALU.mult, ALU.subtract))
                    V(p3_gate3)
                    sv_m2 = C["sv"]

                    def p3_m2t(t, sv_m2=sv_m2, u=u):
                        t.wait_ge(SEM[p3tr_ev[0][0]], p3tr_ev[0][1])
                        t.wait_ge(SEM["sv"], sv_m2)
                        t.transpose(ptr4[0:72, 0:128], M2[:, 0:72], id_s[:]
                                    ).then_inc(SEM["pms"], 1)
                        C["pms"] += 1
                    T(p3_m2t)
                    pms_m2 = C["pms"]

                    def p3_m2tev(v, pms_m2=pms_m2):
                        sop(v, "sv", lambda: v.tensor_copy(
                            M2T[:], ptr4[0:72, 0:128]),
                            also=[("pms", pms_m2)])
                        p3tr_ev[0] = ("sv", C["sv"])
                    V(p3_m2tev)
                    sv_m2T = C["sv"]

                    def p3_mm2(t, u=u, sv_m2T=sv_m2T):
                        t.wait_ge(SEM["sv"], sv_m2T)
                        for fc in range(4):
                            t.wait_ge(SEM["sv"], po_evac[(u, fc)])
                            mi = t.matmul(pob[fc][:], M2T[:],
                                          g_s[:, fc * 512:(fc + 1) * 512],
                                          start=True, stop=True)
                        mi.then_inc(SEM["pm"], 1)
                        C["pm"] += 1
                    T(p3_mm2)
                    pm_mm2 = C["pm"]

                    for fc in range(4):
                        def p3_ffnout(v, u=u, fc=fc, pm_mm2=pm_mm2):
                            also = [("pm", pm_mm2)] if fc == 0 else []
                            sop(v, "sv", lambda: v.scalar_tensor_tensor(
                                ffn[:, fc * 512:(fc + 1) * 512], pob[fc][:], mx[:],
                                thg[:, fc * 512:(fc + 1) * 512], ALU.mult,
                                ALU.subtract), also=also)
                            po_evac[(u, fc)] = C["sv"]
                        V(p3_ffnout)

                        def p3_fch(a, u=u, fc=fc):
                            sop(a, "sa", lambda: a.activation(
                                h2b[:, fc * 512:(fc + 1) * 512],
                                ffn[:, fc * 512:(fc + 1) * 512], AF.Square,
                                accum_out=ssq4[:, fc:fc + 1]),
                                also=[("sv", po_evac[(u, fc)])])
                            sop(a, "sa", lambda: a.activation(
                                h2b[:, fc * 512:(fc + 1) * 512],
                                ffn[:, fc * 512:(fc + 1) * 512], AF.Identity,
                                accum_out=s14[:, fc:fc + 1]))
                        A(p3_fch)
                    ffn_evac[u] = po_evac[(u, 3)]
                    sa_ln3 = ln_combine(C["sa"])

                    def p3_out(v, sa_ln3=sa_ln3):
                        sop(v, "sv", lambda: v.tensor_scalar(thg[:], ffn[:], mu[:],
                                                             rstd[:], ALU.subtract,
                                                             ALU.mult),
                            also=[("sa", sa_ln3)])
                    V(p3_out)
                    sv_thg3 = C["sv"]

                    def p3_outg(g, sv_thg3=sv_thg3, u=u):
                        also = [("sv", sv_thg3)]
                        if u >= 1:
                            also.append(("ost", 16 * u))
                        sop(g, "sg", lambda: g.tensor_add(ot[:], x2[:], thg[:]),
                            also=also)
                    G(p3_outg)
                    p3_done.append((C["sv"], C["sg"]))

                    def p3_store(s, u=u):
                        s.wait_ge(SEM["sg"], p3_done[u][1])
                        s.dma_start(out=y[u * 128:(u + 1) * 128, :], in_=ot[:]
                                    ).then_inc(SEM["ost"], 16)
                        C["ost"] += 16
                    SY(p3_store)

                if taps:
                    def p3_taps(s):
                        s.wait_ge(SEM["ost"], C["ost"])
                        if "qkv" in tap_out:
                            s.dma_start(out=tap_out["qkv"], in_=qkv_raw
                                        ).then_inc(SEM["ost"], 16)
                            C["ost"] += 16

                        if "at" in tap_out:
                            s.dma_start(out=tap_out["at"], in_=arenaB[:]
                                        ).then_inc(SEM["ost"], 16)
                            C["ost"] += 16
                    SY(p3_taps)

                def p3_fin(s):
                    s.wait_ge(SEM["ost"], C["ost"])
                SY(p3_fin)

    return nc


def _host_constants():
    inv_freq = (10000.0 ** (-(np.arange(0, D, 2, dtype=np.float32) / np.float32(D)))
                ).astype(np.float32)
    emb = np.repeat(inv_freq, 2)
    arg = (np.float32(8192.0) * emb).astype(np.float32)
    sinb = np.tile(np.sin(arg).astype(np.float32)[None, :], (128, 1))
    cosb = np.tile(np.cos(arg).astype(np.float32)[None, :], (128, 1))
    return np.ascontiguousarray(sinb), np.ascontiguousarray(cosb)


def make_in_maps(x, Wqkv, bqkv, Wout, bout, sw, md, ic, isl, lu, ld, bw, dw):
    WqkvT = np.ascontiguousarray(Wqkv.T)
    WoutT = np.ascontiguousarray(Wout.T)
    sinb, cosb = _host_constants()
    sin2 = np.ascontiguousarray(np.concatenate([sinb, sinb], axis=1))
    cos2 = np.ascontiguousarray(np.concatenate([cosb, cosb], axis=1))
    identb = np.eye(128, dtype=np.float32).astype(ml_dtypes.bfloat16)
    onesb = np.ones((128, 128), dtype=np.float32).astype(ml_dtypes.bfloat16)
    Wcat = np.ascontiguousarray(np.concatenate([
        np.transpose(sw, (1, 0, 2)).reshape(D, 64),
        np.transpose(lu, (1, 0, 2)).reshape(D, 64),
        md.T, dw.T], axis=1), dtype=np.float32)
    Gmat = np.ascontiguousarray(np.concatenate(
        [np.transpose(ld, (0, 2, 1)).reshape(64, D), bw], axis=0), dtype=np.float32)
    slope_t = np.ascontiguousarray(np.tile(isl[None, :], (128, 1)).astype(np.float32))
    center_t = np.ascontiguousarray(np.tile(ic[None, :], (128, 1)).astype(np.float32))

    in_maps = []
    for core in range(8):
        b, r = core // 4, core % 4
        u = np.arange(128)
        w = np.arange(128)
        mask4 = np.stack([(4 * w[:, None] + o) <= (4 * u[None, :] + r)
                          for o in range(4)])
        masks = mask4.astype(np.float32).astype(ml_dtypes.bfloat16)
        in_maps.append(dict(
            x_sh=np.ascontiguousarray(x[b, r::4, :]), WqkvT=WqkvT,
            bqkv=np.ascontiguousarray(bqkv[None, :]), WoutTd=WoutT,
            bout=np.ascontiguousarray(bout[None, :]), sin2d=sin2, cos2d=cos2,
            masksd=np.ascontiguousarray(masks), identd=identb, onesd=onesb,
            WcatKd=Wcat, Gmatd=Gmat, slopd=slope_t, centd=center_t,
        ))
    return in_maps


def kernel(L0_attn_x, L0_attn_Wqkv, L0_attn_bqkv, L0_attn_Wout, L0_attn_bout,
           L0_ffn_select_weight, L0_ffn_select_mean_diff, L0_ffn_imp_center,
           L0_ffn_imp_slope, L0_ffn_lora_update, L0_ffn_lora_degen,
           L0_ffn_bias_w, L0_ffn_debias_w):
    args = [np.asarray(a, dtype=np.float32) for a in
            (L0_attn_x, L0_attn_Wqkv, L0_attn_bqkv, L0_attn_Wout, L0_attn_bout,
             L0_ffn_select_weight, L0_ffn_select_mean_diff, L0_ffn_imp_center,
             L0_ffn_imp_slope, L0_ffn_lora_update, L0_ffn_lora_degen,
             L0_ffn_bias_w, L0_ffn_debias_w)]
    in_maps = make_in_maps(*args)
    nc = build_program(DEBUG_TAPS)
    res = run_bass_kernel_spmd(nc, in_maps, core_ids=list(range(8)))
    out = np.empty((2, 8192, D), dtype=np.float32)
    for core in range(8):
        b, r = core // 4, core % 4
        out[b, r::4, :] = res.results[core]["y"]
    if DEBUG_TAPS:
        kernel.tap_results = res.results
    return out


if __name__ == "__main__":
    nc = build_program()
    n = sum(1 for _ in nc.all_instructions())
    print("instructions:", n)


# version 13
# speedup vs baseline: 1.1129x; 1.1129x over previous
import contextlib
import numpy as np
import ml_dtypes
import concourse.bass as bass
import concourse.mybir as mybir
from concourse.bass_utils import run_bass_kernel_spmd

F32 = mybir.dt.float32
BF16 = mybir.dt.bfloat16
AF = mybir.ActivationFunctionType
ALU = mybir.AluOpType
AX = mybir.AxisListType

D = 2048
H = 16
OWN = 2048
NT = 16
FQKV = 6144
EPS = 1e-5
SCALE = float(D) ** (-0.25)

DEBUG_TAPS = ()


def build_program(taps=()):
    nc = bass.Bass("TRN2", target_bir_lowering=False, debug=False)

    x_sh = nc.dram_tensor("x_sh", [OWN, D], F32, kind="ExternalInput").ap()
    WqkvT = nc.dram_tensor("WqkvT", [D, FQKV], F32, kind="ExternalInput").ap()
    bqkv = nc.dram_tensor("bqkv", [1, FQKV], F32, kind="ExternalInput").ap()
    WoutTd = nc.dram_tensor("WoutTd", [D, D], F32, kind="ExternalInput").ap()
    bout = nc.dram_tensor("bout", [1, D], F32, kind="ExternalInput").ap()
    sin2d = nc.dram_tensor("sin2d", [128, 2 * D], F32, kind="ExternalInput").ap()
    cos2d = nc.dram_tensor("cos2d", [128, 2 * D], F32, kind="ExternalInput").ap()
    masksd = nc.dram_tensor("masksd", [4, 128, 128], BF16, kind="ExternalInput").ap()
    identd = nc.dram_tensor("identd", [128, 128], BF16, kind="ExternalInput").ap()
    onesd = nc.dram_tensor("onesd", [128, 128], BF16, kind="ExternalInput").ap()
    WcatKd = nc.dram_tensor("WcatKd", [D, 144], F32, kind="ExternalInput").ap()
    Gmatd = nc.dram_tensor("Gmatd", [72, D], F32, kind="ExternalInput").ap()
    slopd = nc.dram_tensor("slopd", [128, 8], F32, kind="ExternalInput").ap()
    centd = nc.dram_tensor("centd", [128, 8], F32, kind="ExternalInput").ap()

    y = nc.dram_tensor("y", [OWN, D], F32, kind="ExternalOutput").ap()
    tap_out = {}
    if "qkv" in taps:
        tap_out["qkv"] = nc.dram_tensor("t_qkv", [NT, 128, FQKV], BF16,
                                        kind="ExternalOutput").ap()
    if "gath" in taps:
        tap_out["gath"] = nc.dram_tensor("t_gath", [NT, 4, 2, H, 128, 128], BF16,
                                         kind="ExternalOutput").ap()
    if "at" in taps:
        tap_out["at"] = nc.dram_tensor("t_at", [128, 32768], BF16,
                                       kind="ExternalOutput").ap()

    qkv_raw = nc.dram_tensor("qkv_raw", [NT, 128, FQKV], BF16).ap()
    shardK = nc.dram_tensor("shardK", [NT, H, 128, 128], BF16).ap()
    shardV = nc.dram_tensor("shardV", [NT, H, 128, 128], BF16).ap()
    gathK = nc.dram_tensor("gathK", [NT, 4, H, 128, 128], BF16).ap()
    gathV = nc.dram_tensor("gathV", [NT, 4, H, 128, 128], BF16).ap()
    crow_d = nc.dram_tensor("crow_d", [136], F32).ap()

    groups = [[0, 1, 2, 3], [4, 5, 6, 7]]

    ctx = contextlib.ExitStack()
    with ctx:
        arenaA = ctx.enter_context(nc.sbuf_tensor("arenaA", [128, 32768], BF16))
        id_s = ctx.enter_context(nc.sbuf_tensor("id_s", [128, 128], BF16))
        ones_s = ctx.enter_context(nc.sbuf_tensor("ones_s", [128, 128], BF16))
        wcat_s = ctx.enter_context(nc.sbuf_tensor("wcat_s", [128, 16 * 144], BF16))
        g_s = ctx.enter_context(nc.sbuf_tensor("g_s", [72, D], BF16))
        bo_bf = ctx.enter_context(nc.sbuf_tensor("bo_bf", [1, D], BF16))
        slope_s = ctx.enter_context(nc.sbuf_tensor("slope_s", [128, 8], F32))
        center_s = ctx.enter_context(nc.sbuf_tensor("center_s", [128, 8], F32))
        ccu_b = ctx.enter_context(nc.sbuf_tensor("ccu_b", [128, 128], F32))
        mm_b = ctx.enter_context(nc.sbuf_tensor("mm_b", [128, 8], F32))
        s1 = ctx.enter_context(nc.sbuf_tensor("s1", [128, 1], F32))
        ssq = ctx.enter_context(nc.sbuf_tensor("ssq", [128, 1], F32))
        mu = ctx.enter_context(nc.sbuf_tensor("mu", [128, 1], F32))
        var = ctx.enter_context(nc.sbuf_tensor("var", [128, 1], F32))
        rstd = ctx.enter_context(nc.sbuf_tensor("rstd", [128, 1], F32))
        tmp1 = ctx.enter_context(nc.sbuf_tensor("tmp1", [128, 1], F32))
        ssh2 = ctx.enter_context(nc.sbuf_tensor("ssh2", [128, 1], F32))
        sq_dump = ctx.enter_context(nc.sbuf_tensor("sq_dump", [128, D], F32))

        block = ctx.enter_context(nc.Block())

        SEM, C = {}, {}
        for name in (["sv", "sa", "sg", "pm", "pms", "pmv", "xld0", "xld1",
                      "wld0", "wld1", "qst0", "qst1", "qst2", "qst3",
                      "qld0", "qld1", "shk", "shv",
                      "w2ld", "ost", "pre", "preg", "scl"]
                     + [f"kld{i}" for i in range(8)]
                     + [f"vld{i}" for i in range(8)]
                     + [f"cc{i}" for i in range(16)]
                     + [f"cv{i}" for i in range(16)]):
            SEM[name] = ctx.enter_context(nc.semaphore(name))
            C[name] = 0

        def bump(inst, name, v=1):
            inst.then_inc(SEM[name], v)
            C[name] += v
            return C[name]

        def sop(e, sname, emit, also=()):
            for (nm, val) in also:
                if val:
                    e.wait_ge(SEM[nm], val)
            e.wait_ge(SEM[sname], C[sname])
            return bump(emit(), sname)

        V = block.vector
        A = block.scalar
        G = block.gpsimd
        T = block.tensor
        SY = block.sync

        from concourse.hw_specs import get_activation_tables
        _setidx = list(get_activation_tables(nc.m.arch).keys()).index(
            "natural_log_exp_and_others")

        def p0_act(a):
            a.add_instruction(mybir.InstLoadActFuncSet(
                name=nc.get_next_instruction_name(),
                act_func_set_id=_setidx, ins=[], outs=[]))
        A(p0_act)

        def p0_loads(s):
            for dst, src in [(id_s[:], identd), (ones_s[:], onesd),
                             (slope_s[:], slopd), (center_s[:], centd)]:
                s.dma_start(out=dst, in_=src).then_inc(SEM["pre"], 16)
                C["pre"] += 16
        SY(p0_loads)

        def p0_cast(g):
            g.dma_start(out=wcat_s[:].rearrange("p (c f) -> p c f", c=16),
                        in_=WcatKd.rearrange("(c p) f -> p c f", p=128)
                        ).then_inc(SEM["preg"], 16)
            g.dma_start(out=g_s[:], in_=Gmatd).then_inc(SEM["preg"], 16)
            g.dma_start(out=bo_bf[:], in_=bout).then_inc(SEM["preg"], 16)
            C["preg"] += 48
        G(p0_cast)
        pre_all = C["pre"]
        preg_all = C["preg"]

        pre_ps = contextlib.ExitStack()
        with pre_ps:
            ps_c = pre_ps.enter_context(nc.psum_tensor("ps_c", [128, 512], F32))
            ps_cu = pre_ps.enter_context(nc.psum_tensor("ps_cu", [128, 512], F32))
            ps_mm = pre_ps.enter_context(nc.psum_tensor("ps_mm", [128, 512], F32))
            crow_s = pre_ps.enter_context(nc.sbuf_tensor("crow_s", [1, 144], F32))

            def p0_ffnconst(t):
                t.wait_ge(SEM["pre"], pre_all)
                t.wait_ge(SEM["preg"], preg_all)
                first = [True, True, True]
                for n in range(8):
                    for c in range(16):
                        base = c * 144
                        last = (n == 7 and c == 15)
                        t.matmul(ps_c[0:1, n * 8:(n + 1) * 8],
                                 wcat_s[:, base + 128 + n:base + 129 + n],
                                 wcat_s[:, base + n * 8:base + (n + 1) * 8],
                                 start=first[0], stop=last)
                        t.matmul(ps_cu[0:1, n * 8:(n + 1) * 8],
                                 wcat_s[:, base + 136 + n:base + 137 + n],
                                 wcat_s[:, base + 64 + n * 8:base + 64 + (n + 1) * 8],
                                 start=first[1], stop=last)
                        mi = t.matmul(ps_mm[0:1, n:n + 1],
                                      wcat_s[:, base + 128 + n:base + 129 + n],
                                      wcat_s[:, base + 128 + n:base + 129 + n],
                                      start=first[2], stop=last)
                        first = [False, False, False]
                mi.then_inc(SEM["pm"], 1)
                C["pm"] += 1
            T(p0_ffnconst)
            pm_ffn = C["pm"]

            def p0_evac(v):
                sop(v, "sv", lambda: v.tensor_copy(crow_s[0:1, 0:64], ps_c[0:1, 0:64]),
                    also=[("pm", pm_ffn)])
                sop(v, "sv", lambda: v.tensor_copy(crow_s[0:1, 64:128], ps_cu[0:1, 0:64]))
                sop(v, "sv", lambda: v.tensor_copy(crow_s[0:1, 128:136], ps_mm[0:1, 0:8]))
            V(p0_evac)
            sv_crow = C["sv"]

            def p0_bounce(s):
                s.wait_ge(SEM["sv"], sv_crow)
                s.dma_start(out=crow_d.unsqueeze(0), in_=crow_s[0:1, 0:136]
                            ).then_inc(SEM["pre"], 16)
                C["pre"] += 16
                s.wait_ge(SEM["pre"], C["pre"])
                s.dma_start(out=ccu_b[:],
                            in_=crow_d[0:128].unsqueeze(0).broadcast_to([128, 128])
                            ).then_inc(SEM["pre"], 16)
                s.dma_start(out=mm_b[:],
                            in_=crow_d[128:136].unsqueeze(0).broadcast_to([128, 8])
                            ).then_inc(SEM["pre"], 16)
                C["pre"] += 32
            SY(p0_bounce)
            pre_done = C["pre"]

        p1a = contextlib.ExitStack()
        with p1a:
            xs0 = p1a.enter_context(nc.sbuf_tensor("xs0", [128, D], F32))
            xs1 = p1a.enter_context(nc.sbuf_tensor("xs1", [128, D], F32))
            h_bf = p1a.enter_context(nc.sbuf_tensor("h_bf", [128, D], BF16))
            ptr0 = p1a.enter_context(nc.psum_tensor("ptr0", [128, 1024], BF16))
            ptr1 = p1a.enter_context(nc.psum_tensor("ptr1", [128, 1024], BF16))
            xbufs = [xs0, xs1]
            ptrs = [ptr0, ptr1]

            sv_affine_done = []
            tr_evac = [None, None]
            for l in range(NT):
                xb = xbufs[l % 2]

                xsem = f"xld{l % 2}"

                def p1a_load(s, l=l, xb=xb, xsem=xsem):
                    if l >= 2:
                        s.wait_ge(SEM["sv"], sv_affine_done[l - 2])
                    s.dma_start(out=xb[:], in_=x_sh[l * 128:(l + 1) * 128, :]
                                ).then_inc(SEM[xsem], 16)
                    C[xsem] += 16
                SY(p1a_load)
                xld_t = C[xsem]

                def p1a_sum(v, xb=xb, xld_t=xld_t, xsem=xsem):
                    sop(v, "sv", lambda: v.reduce_sum(s1[:], xb[:], axis=AX.X),
                        also=[(xsem, xld_t)])
                    sop(v, "sv", lambda: v.tensor_scalar(mu[:], s1[:], 1.0 / D, None, ALU.mult))
                V(p1a_sum)

                def p1a_sq(a, xb=xb, xld_t=xld_t, xsem=xsem):
                    sop(a, "sa", lambda: a.activation(sq_dump[:], xb[:], AF.Square,
                                                      accum_out=ssq[:]),
                        also=[(xsem, xld_t)])
                A(p1a_sq)
                sa_ss = C["sa"]

                def p1a_var(v, sa_ss=sa_ss):
                    sop(v, "sv", lambda: v.tensor_scalar(var[:], ssq[:], 1.0 / D, None, ALU.mult),
                        also=[("sa", sa_ss)])
                    sop(v, "sv", lambda: v.tensor_mul(tmp1[:], mu[:], mu[:]))
                    sop(v, "sv", lambda: v.tensor_sub(var[:], var[:], tmp1[:]))
                V(p1a_var)
                sv_var = C["sv"]

                def p1a_rstd(a, sv_var=sv_var):
                    sop(a, "sa", lambda: a.activation(rstd[:], var[:], AF.Ln,
                                                      bias=np.float32(EPS)),
                        also=[("sv", sv_var)])
                    sop(a, "sa", lambda: a.activation(rstd[:], rstd[:], AF.Exp,
                                                      scale=np.float32(-0.5)))
                A(p1a_rstd)
                sa_rstd = C["sa"]

                def p1a_aff(v, xb=xb, sa_rstd=sa_rstd):
                    sop(v, "sv", lambda: v.tensor_scalar(h_bf[:], xb[:], mu[:], rstd[:],
                                                         ALU.subtract, ALU.mult),
                        also=[("sa", sa_rstd)])
                V(p1a_aff)
                sv_affine_done.append(C["sv"])

                for c in range(16):
                    pt = ptrs[c % 2]

                    def p1a_tr(t, c=c, pt=pt, l=l):
                        t.wait_ge(SEM["sv"], sv_affine_done[l])
                        if l == 0 and c == 0:
                            t.wait_ge(SEM["pre"], pre_all)
                            t.wait_ge(SEM["preg"], preg_all)
                            t.wait_ge(SEM["sv"], sv_crow)
                        if tr_evac[c % 2] is not None:
                            t.wait_ge(SEM["sv"], tr_evac[c % 2])
                        t.transpose(pt[:, 0:128], h_bf[:, c * 128:(c + 1) * 128],
                                    id_s[:]).then_inc(SEM["pms"], 1)
                        C["pms"] += 1
                    T(p1a_tr)
                    pms_t = C["pms"]

                    def p1a_trev(v, c=c, pt=pt, l=l, pms_t=pms_t):
                        sop(v, "sv", lambda: v.tensor_copy(
                            arenaA[:, c * 2048 + l * 128: c * 2048 + (l + 1) * 128],
                            pt[:, 0:128]), also=[("pms", pms_t)])
                    V(p1a_trev)
                    tr_evac[c % 2] = C["sv"]
            sv_hT_done = C["sv"]

        p1b = contextlib.ExitStack()
        with p1b:
            bq_bf = p1b.enter_context(nc.sbuf_tensor("bq_bf", [1, FQKV], BF16))
            wsl0 = p1b.enter_context(nc.sbuf_tensor("wsl0", [128, 16 * 512], BF16))
            wsl1 = p1b.enter_context(nc.sbuf_tensor("wsl1", [128, 16 * 512], BF16))
            evb = [p1b.enter_context(nc.sbuf_tensor(f"ev{i}", [128, 512], BF16))
                   for i in range(4)]
            pqb = [p1b.enter_context(nc.psum_tensor(f"pq{i}", [128, 512], F32))
                   for i in range(4)]
            wslabs = [wsl0, wsl1]

            pm_qkv = {}
            sv_ev = {}
            qst_d = {}
            shv_tile_d = {}
            wld_after = []
            for fc in range(12):
                wb = wslabs[fc % 2]

                wsem = f"wld{fc % 2}"

                def p1b_wload(g, fc=fc, wb=wb, wsem=wsem):
                    if fc == 0:
                        g.wait_ge(SEM["sv"], sv_hT_done)
                        g.dma_start(out=bq_bf[:], in_=bqkv).then_inc(SEM["preg"], 16)
                        C["preg"] += 16
                    if fc >= 2:
                        g.wait_ge(SEM["pm"], pm_qkv[(fc - 2, NT - 1)])
                    src = WqkvT[:, fc * 512:(fc + 1) * 512].rearrange(
                        "(c p) f -> p c f", p=128)
                    dst = wb[:].rearrange("p (c f) -> p c f", c=16)
                    for q in range(4):
                        g.dma_start(out=dst[:, q * 4:(q + 1) * 4, :],
                                    in_=src[:, q * 4:(q + 1) * 4, :]
                                    ).then_inc(SEM[wsem], 16)
                        C[wsem] += 16
                G(p1b_wload)
                wld_after.append((wsem, C[wsem]))

                for tt in range(NT):
                    pq = pqb[tt % 4]
                    eb = evb[tt % 4]
                    key = (fc, tt - 4) if tt >= 4 else (fc - 1, tt + 12)

                    def p1b_mm(t, fc=fc, tt=tt, pq=pq, key=key):
                        t.wait_ge(SEM[wld_after[fc][0]], wld_after[fc][1])
                        if tt == 0:
                            t.wait_ge(SEM["preg"], preg_all + 16)
                        if fc == 0 and tt == 0:
                            t.wait_ge(SEM["sv"], sv_hT_done)
                            t.wait_ge(SEM["pre"], pre_done)
                        if key in sv_ev:
                            t.wait_ge(SEM["sv"], sv_ev[key])
                        t.matmul(pq[:], ones_s[0:1, :],
                                 bq_bf[0:1, fc * 512:(fc + 1) * 512],
                                 start=True, stop=False)
                        for c in range(16):
                            mi = t.matmul(pq[:],
                                          arenaA[:, c * 2048 + tt * 128: c * 2048 + (tt + 1) * 128],
                                          wb[:, c * 512:(c + 1) * 512],
                                          start=False, stop=(c == 15))
                        mi.then_inc(SEM["pm"], 1)
                        C["pm"] += 1
                    T(p1b_mm)
                    pm_qkv[(fc, tt)] = C["pm"]

                    qsem = f"qst{tt % 4}"

                    def p1b_ev(v, fc=fc, tt=tt, pq=pq, eb=eb, key=key, qsem=qsem):
                        also = [("pm", pm_qkv[(fc, tt)])]
                        if key in qst_d:
                            also.append(qst_d[key])
                        sop(v, "sv", lambda: v.tensor_copy(eb[:], pq[:]), also=also)
                    V(p1b_ev)
                    sv_ev[(fc, tt)] = C["sv"]

                    if fc < 8:
                        def p1b_st(s, fc=fc, tt=tt, eb=eb, qsem=qsem):
                            s.wait_ge(SEM["sv"], sv_ev[(fc, tt)])
                            s.dma_start(out=qkv_raw[tt, :, fc * 512:(fc + 1) * 512],
                                        in_=eb[:]).then_inc(SEM[qsem], 16)
                            C[qsem] += 16
                        SY(p1b_st)
                        qst_d[(fc, tt)] = (qsem, C[qsem])
                    else:
                        def p1b_vst(sc, fc=fc, tt=tt, eb=eb):
                            sc.wait_ge(SEM["sv"], sv_ev[(fc, tt)])
                            if (fc, tt) != (8, 0):
                                sc.wait_ge(SEM["shv"], C["shv"])
                            c0 = 4 * (fc - 8)
                            sc.dma_start(
                                out=shardV[tt, c0:c0 + 4].rearrange("c w d -> w c d"),
                                in_=eb[:].rearrange("p (c d) -> p c d", c=4)
                            ).then_inc(SEM["shv"], 16)
                            C["shv"] += 16
                        A(p1b_vst)
                        qst_d[(fc, tt)] = ("shv", C["shv"])
                        shv_tile_d[tt] = C["shv"]
                        if fc == 11:
                            def p1b_ccv(g, tt=tt):
                                g.wait_ge(SEM["shv"], shv_tile_d[tt])
                                g.collective_compute(
                                    "AllGather", ALU.bypass, replica_groups=groups,
                                    ins=[shardV[tt].rearrange("c d w -> (c d) w")],
                                    outs=[gathV[tt].rearrange(
                                        "o c d w -> (o c d) w")],
                                ).then_inc(SEM[f"cv{tt}"])
                                C[f"cv{tt}"] += 1
                            G(p1b_ccv)
            qst_all = [(f"qst{i}", C[f"qst{i}"]) for i in range(4)]
            qst_all.append(("shv", C["shv"]))

        p1c = contextlib.ExitStack()
        with p1c:
            sin2 = p1c.enter_context(nc.sbuf_tensor("sin2", [128, 2 * D], F32))
            cos2 = p1c.enter_context(nc.sbuf_tensor("cos2", [128, 2 * D], F32))
            qk0 = p1c.enter_context(nc.sbuf_tensor("qk0", [128, 2 * D], BF16))
            qk1 = p1c.enter_context(nc.sbuf_tensor("qk1", [128, 2 * D], BF16))
            rot4 = p1c.enter_context(nc.sbuf_tensor("rot4", [128, 2 * D], F32))
            rt14 = p1c.enter_context(nc.sbuf_tensor("rt14", [128, 2 * D], F32))
            ssqk = p1c.enter_context(nc.sbuf_tensor("ssqk", [128, 1], F32))
            rstdk = p1c.enter_context(nc.sbuf_tensor("rstdk", [128, 1], F32))
            qf_bf = p1c.enter_context(nc.sbuf_tensor("qf_bf", [128, D], BF16))
            kts = [p1c.enter_context(nc.sbuf_tensor(f"kt{i}", [128, D], BF16))
                   for i in range(2)]
            ptr2 = p1c.enter_context(nc.psum_tensor("ptr2", [128, 1024], BF16))
            ptr3 = p1c.enter_context(nc.psum_tensor("ptr3", [128, 1024], BF16))
            qkbufs = [qk0, qk1]
            ptrs2 = [ptr2, ptr3]

            def p1c_consts(s):
                for nm, val in qst_all:
                    s.wait_ge(SEM[nm], val)
                s.dma_start(out=sin2[:], in_=sin2d).then_inc(SEM["scl"], 16)
                s.dma_start(out=cos2[:], in_=cos2d).then_inc(SEM["scl"], 16)
                C["scl"] += 32
            SY(p1c_consts)
            scl_all = C["scl"]

            sv_p1c_done = []
            shst_k = []
            sv_tile_q = []
            tr2_evac = [None, None]
            for l in range(NT):
                qkb = qkbufs[l % 2]
                ktb = kts[l % 2]
                lsem = f"qld{l % 2}"

                def p1c_load(s, l=l, qkb=qkb, lsem=lsem):
                    if l >= 2:
                        s.wait_ge(SEM["sv"], sv_p1c_done[l - 2][0])
                    s.dma_start(out=qkb[:], in_=qkv_raw[l, :, 0:2 * D]
                                ).then_inc(SEM[lsem], 16)
                    C[lsem] += 16
                SY(p1c_load)
                qld_t = (lsem, C[lsem])

                def p1c_rope(v, qkb=qkb, qld_t=qld_t):
                    qkv2 = qkb[:, 0:2 * D].rearrange("p (f two) -> p f two", two=2)
                    evens = qkv2[:, :, 0:1].squeeze(2).rearrange("p (s x) -> p s x", s=2)
                    odds = qkv2[:, :, 1:2].squeeze(2).rearrange("p (s x) -> p s x", s=2)
                    r4 = rot4[:].rearrange("p (a b) -> p a b", a=4)
                    sop(v, "sv", lambda: v.tensor_scalar(
                        r4[:, 0:3:2, :], odds, -1.0, None, ALU.mult),
                        also=[qld_t, ("scl", scl_all)])
                    sop(v, "sv", lambda: v.tensor_copy(r4[:, 1:4:2, :], evens))
                    sop(v, "sv", lambda: v.tensor_mul(rt14[:], qkb[:, 0:2 * D], cos2[:]))
                    sop(v, "sv", lambda: v.tensor_mul(rot4[:], rot4[:], sin2[:]))
                    sop(v, "sv", lambda: v.tensor_add(rot4[:], rot4[:], rt14[:]))
                V(p1c_rope)
                sv_rope = C["sv"]

                def p1c_acts(a, qkb=qkb, l=l, qld_t=qld_t, sv_rope=sv_rope):
                    sop(a, "sa", lambda: a.activation(sq_dump[:], rot4[:, 0:D],
                                                      AF.Square, accum_out=ssq[:]),
                        also=[("sv", sv_rope)])
                    sop(a, "sa", lambda: a.activation(rstd[:], ssq[:], AF.Ln,
                                                      bias=np.float32(1e-24)))
                    sop(a, "sa", lambda: a.activation(
                        rstd[:], rstd[:], AF.Exp, scale=np.float32(-0.5),
                        bias=np.float32(np.log(SCALE))))
                    sop(a, "sa", lambda: a.activation(sq_dump[:], rot4[:, D:2 * D],
                                                      AF.Square, accum_out=ssqk[:]))
                    sop(a, "sa", lambda: a.activation(rstdk[:], ssqk[:], AF.Ln,
                                                      bias=np.float32(1e-24)))
                    sop(a, "sa", lambda: a.activation(
                        rstdk[:], rstdk[:], AF.Exp, scale=np.float32(-0.5),
                        bias=np.float32(np.log(SCALE))))
                A(p1c_acts)
                sa_rnq = C["sa"] - 3
                sa_rnk = C["sa"]

                def p1c_scales(v, ktb=ktb, l=l, sa_rnq=sa_rnq, sa_rnk=sa_rnk):
                    sop(v, "sv", lambda: v.tensor_scalar(qf_bf[:], rot4[:, 0:D],
                                                         rstd[:], None, ALU.mult),
                        also=[("sa", sa_rnq)])
                    also_k = [("sa", sa_rnk)]
                    if l >= 2:
                        also_k.append(("shk", shst_k[l - 2]))
                    sop(v, "sv", lambda: v.tensor_scalar(ktb[:], rot4[:, D:2 * D],
                                                         rstdk[:], None, ALU.mult),
                        also=also_k)
                V(p1c_scales)
                sv_kf = C["sv"]
                sv_p1c_done.append((C["sv"], 0))

                for c in range(16):
                    pt = ptrs2[c % 2]

                    def p1c_trq(t, c=c, pt=pt, l=l, sv_kf=sv_kf):
                        t.wait_ge(SEM["sv"], sv_kf)
                        if tr2_evac[c % 2] is not None:
                            t.wait_ge(SEM[tr2_evac[c % 2][0]], tr2_evac[c % 2][1])
                        t.transpose(pt[:, 0:128], qf_bf[:, c * 128:(c + 1) * 128],
                                    id_s[:]).then_inc(SEM["pms"], 1)
                        C["pms"] += 1
                    T(p1c_trq)
                    pms_q = C["pms"]

                    def p1c_trqev(a, c=c, pt=pt, l=l, pms_q=pms_q):
                        sop(a, "sa", lambda: a.activation(
                            arenaA[:, c * 2048 + l * 128: c * 2048 + (l + 1) * 128],
                            pt[:, 0:128], AF.Identity), also=[("pms", pms_q)])
                    A(p1c_trqev)
                    tr2_evac[c % 2] = ("sa", C["sa"])
                sv_tile_q.append(C["sv"])

                for c in range(16):
                    pt = ptrs2[c % 2]

                    def p1c_trk(t, c=c, pt=pt, ktb=ktb, sv_kf=sv_kf):
                        t.wait_ge(SEM["sv"], sv_kf)
                        t.wait_ge(SEM[tr2_evac[c % 2][0]], tr2_evac[c % 2][1])
                        t.transpose(pt[:, 0:128], ktb[:, c * 128:(c + 1) * 128],
                                    id_s[:]).then_inc(SEM["pms"], 1)
                        C["pms"] += 1
                    T(p1c_trk)
                    pms_k = C["pms"]

                    def p1c_trkev(a, c=c, pt=pt, ktb=ktb, pms_k=pms_k):
                        sop(a, "sa", lambda: a.activation(
                            ktb[:, c * 128:(c + 1) * 128], pt[:, 0:128],
                            AF.Identity), also=[("pms", pms_k)])
                    A(p1c_trkev)
                    tr2_evac[c % 2] = ("sa", C["sa"])
                sa_kt = C["sa"]

                def p1c_shk(sc, l=l, ktb=ktb, sa_kt=sa_kt):
                    sc.wait_ge(SEM["sa"], sa_kt)
                    if l >= 1:
                        sc.wait_ge(SEM["shk"], shst_k[l - 1])
                    sc.dma_start(out=shardK[l].rearrange("c d w -> d c w"),
                                 in_=ktb[:].rearrange("p (c w) -> p c w", c=16)
                                 ).then_inc(SEM["shk"], 16)
                    C["shk"] += 16
                A(p1c_shk)
                shst_k.append(C["shk"])

                def p1c_cc(g, l=l):
                    g.wait_ge(SEM["shk"], shst_k[l])
                    g.collective_compute(
                        "AllGather", ALU.bypass, replica_groups=groups,
                        ins=[shardK[l].rearrange("c d w -> (c d) w")],
                        outs=[gathK[l].rearrange("o c d w -> (o c d) w")],
                    ).then_inc(SEM[f"cc{l}"])
                    C[f"cc{l}"] += 1
                G(p1c_cc)
            sv_p1c_end = C["sv"]
            sa_p1c_end = C["sa"]
            shst_all = (C["shk"], C["shv"])

        p23 = contextlib.ExitStack()
        arenaB = p23.enter_context(nc.sbuf_tensor("arenaB", [128, 32768], BF16))
        with p23:
            p2 = contextlib.ExitStack()
            with p2:
                mask_s = p2.enter_context(nc.sbuf_tensor("mask_s", [128, 512], BF16))
                kbuf = p2.enter_context(nc.sbuf_tensor("kbuf", [128, 8 * 512], BF16))
                vbuf = p2.enter_context(nc.sbuf_tensor("vbuf", [128, 8 * 512], BF16))
                PT = p2.enter_context(nc.sbuf_tensor("PT", [128, 3072], BF16))
                rdn = p2.enter_context(nc.sbuf_tensor("rdn", [128, 512], F32))
                ST = p2.enter_context(nc.psum_tensor("ST", [128, 3072], F32))
                AT = p2.enter_context(nc.psum_tensor("AT", [128, 512], F32))
                DN = p2.enter_context(nc.psum_tensor("DN", [128, 512], F32))

                kv_sched = []
                for h in range(H):
                    for p in range(4):
                        for l in range(4 * p + 4):
                            kv_sched.append((h, p, l))

                pm_visit = []
                exp_sa = {}
                pmv_pair = {}
                mask_cnt = {}
                sv_p2_0 = C["sv"]
                sa_p2_0 = C["sa"]
                atden_free = (sv_p2_0, sa_p2_0)

                def emit_kv_dma(s_idx):
                    hh, pp, ll = kv_sched[s_idx]
                    par = s_idx % 8

                    def f(sy, hh=hh, ll=ll, par=par, s_idx=s_idx):
                        if s_idx < 8:
                            sy.wait_ge(SEM["sv"], sv_p1c_end)
                            sy.wait_ge(SEM["shk"], shst_all[0])
                            sy.wait_ge(SEM["shv"], shst_all[1])
                        sy.wait_ge(SEM[f"cc{ll}"], 1)
                        sy.wait_ge(SEM[f"cv{ll}"], 1)
                        if s_idx >= 8:
                            sy.wait_ge(SEM["pmv"], pm_visit[s_idx - 8])
                        sy.dma_start(
                            out=kbuf[:, par * 512:(par + 1) * 512]
                            .rearrange("p (o w) -> p o w", o=4),
                            in_=gathK[ll, :, hh].rearrange("o d w -> d o w")
                        ).then_inc(SEM[f"kld{par}"], 16)
                        C[f"kld{par}"] += 16
                        sy.dma_start(
                            out=vbuf[:, par * 512:(par + 1) * 512]
                            .rearrange("p (o d) -> p o d", o=4),
                            in_=gathV[ll, :, hh].rearrange("o w d -> w o d")
                        ).then_inc(SEM[f"vld{par}"], 16)
                        C[f"vld{par}"] += 16
                    SY(f)

                def p2_mload(s):
                    s.wait_ge(SEM["sv"], sv_p1c_end)
                    s.dma_start(out=mask_s[:].rearrange("p (o w) -> p o w", o=4),
                                in_=masksd.rearrange("o p w -> p o w")
                                ).then_inc(SEM["scl"], 16)
                    C["scl"] += 16
                SY(p2_mload)
                scl_mask = C["scl"]

                for _s in range(min(7, len(kv_sched))):
                    emit_kv_dma(_s)

                visit = 0
                pass_idx = 0

                def sched_params(s_idx):
                    hh, pp, ll = kv_sched[s_idx]
                    i_lo = 128 * (ll - 4 * pp) if ll >= 4 * pp else 0
                    return (hh, pp, ll, i_lo, hh * 2048 + 512 * pp)

                def emit_s_block(s_idx):
                    hh, pp, ll, i_lo, qbase = sched_params(s_idx)
                    par = s_idx % 8
                    kv_need = 16 * (s_idx // 8 + 1)
                    pairs = (2 * s_idx, 2 * s_idx + 1)

                    def p2_s(t, par=par, i_lo=i_lo, qbase=qbase,
                             kv_need=kv_need, s_idx=s_idx, pairs=pairs):
                        t.wait_ge(SEM[f"kld{par}"], kv_need)
                        if s_idx == 0:
                            t.wait_ge(SEM["sv"], sv_p1c_end)
                            t.wait_ge(SEM["sa"], sa_p1c_end)
                        for cp, pi in enumerate(pairs):
                            slot = pi % 3
                            if pi - 3 in exp_sa:
                                t.wait_ge(SEM["sa"], exp_sa[pi - 3])
                            for jb2 in (0, 1):
                                jb = 2 * cp + jb2
                                mi = t.matmul(
                                    ST[:, slot * 1024 + jb2 * 512 + i_lo:
                                       slot * 1024 + (jb2 + 1) * 512],
                                    kbuf[:, par * 512 + jb * 128:par * 512 + (jb + 1) * 128],
                                    arenaA[:, qbase + i_lo:qbase + 512],
                                    start=True, stop=True)
                            mi.then_inc(SEM["pms"], 1)
                            C["pms"] += 1
                    T(p2_s)
                    pms_sblock[s_idx] = (C["pms"] - 1, C["pms"])

                pms_sblock = {}
                emit_s_block(0)

                for h in range(H):
                    for p in range(4):
                        L = 4 * p + 4
                        for l in range(L):
                            s_idx = visit
                            par = s_idx % 8
                            i_lo = 128 * (l - 4 * p) if l >= 4 * p else 0
                            diag = l >= 4 * p
                            kv_need = 16 * (s_idx // 8 + 1)
                            first_pv = (l == 0)
                            pairs = (2 * s_idx, 2 * s_idx + 1)
                            pms_pairs = pms_sblock[s_idx]

                            if s_idx + 1 < len(kv_sched):
                                emit_s_block(s_idx + 1)

                            for cp, pi in enumerate(pairs):
                                slot = pi % 3

                                def p2_exp(a, cp=cp, pi=pi, slot=slot, i_lo=i_lo,
                                           pp2=pms_pairs[cp]):
                                    a.wait_ge(SEM["pms"], pp2)
                                    if pi - 3 in pmv_pair:
                                        a.wait_ge(SEM["pmv"], pmv_pair[pi - 3])
                                    sop(a, "sa", lambda: a.activation(
                                        PT[:, slot * 1024 + i_lo:(slot + 1) * 1024],
                                        ST[:, slot * 1024 + i_lo:(slot + 1) * 1024],
                                        AF.Exp))
                                A(p2_exp)
                                exp_sa[pi] = C["sa"]

                                if diag:
                                    for jb2 in (0, 1):
                                        def p2_mask(g, jb2=jb2, cp=cp, slot=slot,
                                                    i_lo=i_lo, pi=pi):
                                            jb = 2 * cp + jb2
                                            off = slot * 1024 + jb2 * 512 + i_lo
                                            sop(g, "sg", lambda: g.tensor_mul(
                                                PT[:, off:off + 128],
                                                PT[:, off:off + 128],
                                                mask_s[:, jb * 128:(jb + 1) * 128]),
                                                also=[("sa", exp_sa[pi]),
                                                      ("scl", scl_mask)])
                                        G(p2_mask)
                                        mask_cnt[(s_idx, 2 * cp + jb2)] = C["sg"]

                            def p2_pv(t, s_idx=s_idx, par=par, i_lo=i_lo, diag=diag,
                                      first_pv=first_pv, l=l, L=L, kv_need=kv_need,
                                      pairs=pairs):
                                t.wait_ge(SEM[f"vld{par}"], kv_need)
                                if first_pv:
                                    t.wait_ge(SEM["sv"], atden_free[0])
                                    t.wait_ge(SEM["sa"], atden_free[1])
                                if diag:
                                    t.wait_ge(SEM["sg"], mask_cnt[(s_idx, 3)])
                                for cp, pi in enumerate(pairs):
                                    slot = pi % 3
                                    t.wait_ge(SEM["sa"], exp_sa[pi])
                                    for jb2 in (0, 1):
                                        jb = 2 * cp + jb2
                                        st_mm = (l == 0 and jb == 0)
                                        sp_mm = (l == L - 1 and jb == 3)
                                        pt_sl = PT[:, slot * 1024 + jb2 * 512 + i_lo:
                                                   slot * 1024 + (jb2 + 1) * 512]
                                        t.matmul(AT[:, i_lo:512],
                                                 vbuf[:, par * 512 + jb * 128:par * 512 + (jb + 1) * 128],
                                                 pt_sl, start=st_mm, stop=sp_mm)
                                        mi = t.matmul(DN[:, i_lo:512], ones_s[:],
                                                      pt_sl, start=st_mm, stop=sp_mm)
                                    mi.then_inc(SEM["pmv"], 1)
                                    C["pmv"] += 1
                                    pmv_pair[pi] = C["pmv"]
                            T(p2_pv)
                            pm_visit.append(C["pmv"])

                            visit += 1
                            if visit + 6 < len(kv_sched):
                                emit_kv_dma(visit + 6)

                        pm_pass = pm_visit[-1]

                        def p2_rden(a, pm_pass=pm_pass, sv_ev=atden_free[0]):
                            sop(a, "sa", lambda: a.activation(rdn[:], DN[:], AF.Ln),
                                also=[("pmv", pm_pass), ("sv", sv_ev)])
                            sop(a, "sa", lambda: a.activation(rdn[:], rdn[:], AF.Exp,
                                                              scale=np.float32(-1.0)))
                        A(p2_rden)
                        sa_rden = C["sa"]

                        def p2_evac(v, h=h, p=p, pm_pass=pm_pass, sa_rden=sa_rden):
                            sop(v, "sv", lambda: v.tensor_mul(
                                arenaB[:, h * 2048 + 512 * p: h * 2048 + 512 * (p + 1)],
                                AT[:], rdn[:]),
                                also=[("pmv", pm_pass), ("sa", sa_rden)])
                        V(p2_evac)
                        atden_free = (C["sv"], C["sa"])
                        pass_idx += 1

                    def p2_wout(g, h=h):
                        g.wait_ge(SEM["pmv"], pm_visit[-1])
                        g.dma_start(
                            out=arenaA[:, h * 2048:(h + 1) * 2048].unsqueeze(1),
                            in_=WoutTd.rearrange("(c p) f -> p c f", p=128)[:, h:h + 1, :]
                        ).then_inc(SEM["w2ld"], 16)
                        C["w2ld"] += 16
                    G(p2_wout)
                sv_at_done = C["sv"]
                sa_p2_end = C["sa"]
                pm_p2_end = C["pmv"]
                w2_all = C["w2ld"]

            p3 = contextlib.ExitStack()
            with p3:
                xs = p3.enter_context(nc.sbuf_tensor("p3xs", [128, D], F32))
                ao = p3.enter_context(nc.sbuf_tensor("p3ao", [128, D], F32))
                x2 = p3.enter_context(nc.sbuf_tensor("p3x2", [128, D], F32))
                h2f = p3.enter_context(nc.sbuf_tensor("p3h2f", [128, D], F32))
                h2b = p3.enter_context(nc.sbuf_tensor("p3h2b", [128, D], BF16))
                h2T = p3.enter_context(nc.sbuf_tensor("p3h2T", [128, D], BF16))
                otb = p3.enter_context(nc.sbuf_tensor("p3ot", [128, D], F32))
                Dc = p3.enter_context(nc.sbuf_tensor("p3Dc", [128, 128], F32))
                D2t = p3.enter_context(nc.sbuf_tensor("p3D2", [128, 64], F32))
                sco = p3.enter_context(nc.sbuf_tensor("p3sco", [128, 8], F32))
                nd = p3.enter_context(nc.sbuf_tensor("p3nd", [128, 8], F32))
                zt = p3.enter_context(nc.sbuf_tensor("p3zt", [128, 8], F32))
                gt = p3.enter_context(nc.sbuf_tensor("p3gt", [128, 8], F32))
                mx = p3.enter_context(nc.sbuf_tensor("p3mx", [128, 1], F32))
                eq = p3.enter_context(nc.sbuf_tensor("p3eq", [128, 8], F32))
                t8a = p3.enter_context(nc.sbuf_tensor("p3t8a", [128, 8], F32))
                t8b = p3.enter_context(nc.sbuf_tensor("p3t8b", [128, 8], F32))
                oh = p3.enter_context(nc.sbuf_tensor("p3oh", [128, 8], F32))
                M2 = p3.enter_context(nc.sbuf_tensor("p3M2", [128, 72], BF16))
                M2T = p3.enter_context(nc.sbuf_tensor("p3M2T", [72, 128], BF16))
                ssq4 = p3.enter_context(nc.sbuf_tensor("p3ssq4", [128, 4], F32))
                s14 = p3.enter_context(nc.sbuf_tensor("p3s14", [128, 4], F32))
                pob = [p3.enter_context(nc.psum_tensor(f"po{i}", [128, 512], F32))
                       for i in range(4)]
                pf = p3.enter_context(nc.psum_tensor("p3pf", [128, 512], F32))
                ptr4 = p3.enter_context(nc.psum_tensor("ptr4", [128, 1024], BF16))
                ptr5 = p3.enter_context(nc.psum_tensor("ptr5", [128, 1024], BF16))
                ffn = xs
                ot = otb
                thg = sq_dump

                p3_done = []
                po_evac = {}
                pf_evac = {}
                ffn_evac = {}
                pm_op_d = {}
                p3tr_ev = [None, None]

                def ln_combine(sa_chunks, extra_ss=False):
                    def fv2(v, sa_chunks=sa_chunks):
                        sop(v, "sv", lambda: v.reduce_sum(s1[:], s14[:], axis=AX.X),
                            also=[("sa", sa_chunks)])
                        sop(v, "sv", lambda: v.reduce_sum(ssq[:], ssq4[:], axis=AX.X))
                        sop(v, "sv", lambda: v.tensor_scalar(mu[:], s1[:], 1.0 / D,
                                                             None, ALU.mult))
                        sop(v, "sv", lambda: v.tensor_scalar(var[:], ssq[:], 1.0 / D,
                                                             None, ALU.mult))
                        sop(v, "sv", lambda: v.tensor_mul(tmp1[:], mu[:], mu[:]))
                        sop(v, "sv", lambda: v.tensor_sub(var[:], var[:], tmp1[:]))
                        if extra_ss:
                            sop(v, "sv", lambda: v.tensor_scalar(tmp1[:], var[:], EPS,
                                                                 None, ALU.add))
                            sop(v, "sv", lambda: v.reciprocal(tmp1[:], tmp1[:]))
                            sop(v, "sv", lambda: v.tensor_scalar(ssh2[:], var[:],
                                                                 float(D), None, ALU.mult))
                            sop(v, "sv", lambda: v.tensor_mul(ssh2[:], ssh2[:], tmp1[:]))
                    V(fv2)
                    sv_v = C["sv"]

                    def fa2(a, sv_v=sv_v):
                        sop(a, "sa", lambda: a.activation(rstd[:], var[:], AF.Ln,
                                                          bias=np.float32(EPS)),
                            also=[("sv", sv_v)])
                        sop(a, "sa", lambda: a.activation(rstd[:], rstd[:], AF.Exp,
                                                          scale=np.float32(-0.5)))
                    A(fa2)
                    return C["sa"]

                def ln_stats(src, extra_ss=False):
                    sv_src = C["sv"]

                    def fa(a, sv_src=sv_src):
                        sop(a, "sa", lambda: a.activation(sq_dump[:], src[:], AF.Square,
                                                          accum_out=ssq[:]),
                            also=[("sv", sv_src)])
                        sop(a, "sa", lambda: a.activation(sq_dump[:], src[:],
                                                          AF.Identity,
                                                          accum_out=s1[:]))
                    A(fa)
                    sa_s = C["sa"]

                    def fv2(v, sa_s=sa_s):
                        sop(v, "sv", lambda: v.tensor_scalar(mu[:], s1[:], 1.0 / D,
                                                             None, ALU.mult),
                            also=[("sa", sa_s)])
                        sop(v, "sv", lambda: v.tensor_scalar(var[:], ssq[:], 1.0 / D,
                                                             None, ALU.mult))
                        sop(v, "sv", lambda: v.tensor_mul(tmp1[:], mu[:], mu[:]))
                        sop(v, "sv", lambda: v.tensor_sub(var[:], var[:], tmp1[:]))
                        if extra_ss:
                            sop(v, "sv", lambda: v.tensor_scalar(tmp1[:], var[:], EPS,
                                                                 None, ALU.add))
                            sop(v, "sv", lambda: v.reciprocal(tmp1[:], tmp1[:]))
                            sop(v, "sv", lambda: v.tensor_scalar(ssh2[:], var[:],
                                                                 float(D), None, ALU.mult))
                            sop(v, "sv", lambda: v.tensor_mul(ssh2[:], ssh2[:], tmp1[:]))
                    V(fv2)
                    sv_v = C["sv"]

                    def fa2(a, sv_v=sv_v):
                        sop(a, "sa", lambda: a.activation(rstd[:], var[:], AF.Ln,
                                                          bias=np.float32(EPS)),
                            also=[("sv", sv_v)])
                        sop(a, "sa", lambda: a.activation(rstd[:], rstd[:], AF.Exp,
                                                          scale=np.float32(-0.5)))
                    A(fa2)
                    return C["sa"]

                def emit_op(u):
                    def p3_op(t, u=u):
                        if u == 0:
                            t.wait_ge(SEM["sv"], sv_at_done)
                            t.wait_ge(SEM["sa"], sa_p2_end)
                            t.wait_ge(SEM["w2ld"], w2_all)
                            t.wait_ge(SEM["preg"], preg_all)
                        for fc in range(4):
                            if (u - 1, fc) in po_evac:
                                t.wait_ge(SEM["sv"], po_evac[(u - 1, fc)])
                            t.matmul(pob[fc][:], ones_s[0:1, :],
                                     bo_bf[0:1, fc * 512:(fc + 1) * 512],
                                     start=True, stop=False)
                            for c in range(16):
                                mi = t.matmul(pob[fc][:],
                                              arenaB[:, c * 2048 + u * 128: c * 2048 + (u + 1) * 128],
                                              arenaA[:, c * 2048 + fc * 512: c * 2048 + (fc + 1) * 512],
                                              start=False, stop=(c == 15))
                        mi.then_inc(SEM["pm"], 1)
                        C["pm"] += 1
                    T(p3_op)
                    pm_op_d[u] = C["pm"]

                for u in range(NT):
                    emit_op(u)
                    xsem = f"xld{u % 2}"

                    def p3_xload(s, u=u, xsem=xsem):
                        if u == 0:
                            s.wait_ge(SEM["pmv"], pm_p2_end)
                        if u >= 1:
                            s.wait_ge(SEM["sv"], p3_done[u - 1][0])
                            s.wait_ge(SEM["sg"], p3_done[u - 1][1])
                        s.dma_start(out=xs[:], in_=x_sh[u * 128:(u + 1) * 128, :]
                                    ).then_inc(SEM[xsem], 16)
                        C[xsem] += 16
                    SY(p3_xload)
                    xld_u = (xsem, C[xsem])

                    for fc in range(4):
                        def p3_aoev(v, u=u, fc=fc):
                            also = [("pm", pm_op_d[u])] if fc == 0 else []
                            sop(v, "sv", lambda: v.tensor_copy(
                                ao[:, fc * 512:(fc + 1) * 512], pob[fc][:]), also=also)
                            po_evac[(u, fc)] = C["sv"]
                        V(p3_aoev)

                        def p3_ach(a, u=u, fc=fc):
                            sop(a, "sa", lambda: a.activation(
                                h2b[:, fc * 512:(fc + 1) * 512],
                                ao[:, fc * 512:(fc + 1) * 512], AF.Square,
                                accum_out=ssq4[:, fc:fc + 1]),
                                also=[("sv", po_evac[(u, fc)])])
                            sop(a, "sa", lambda: a.activation(
                                h2b[:, fc * 512:(fc + 1) * 512],
                                ao[:, fc * 512:(fc + 1) * 512], AF.Identity,
                                accum_out=s14[:, fc:fc + 1]))
                        A(p3_ach)
                    sa_ln1 = ln_combine(C["sa"])

                    def p3_x2(v, xld_u=xld_u, sa_ln1=sa_ln1):
                        sop(v, "sv", lambda: v.tensor_scalar(thg[:], ao[:], mu[:],
                                                             rstd[:], ALU.subtract,
                                                             ALU.mult),
                            also=[("sa", sa_ln1), xld_u])
                    V(p3_x2)
                    sv_thg1 = C["sv"]

                    def p3_x2g(g, sv_thg1=sv_thg1):
                        sop(g, "sg", lambda: g.tensor_add(x2[:], xs[:], thg[:]),
                            also=[("sv", sv_thg1)])
                    G(p3_x2g)
                    sg_x2 = C["sg"]

                    def p3_x2sync(a, sg_x2=sg_x2):
                        a.wait_ge(SEM["sg"], sg_x2)
                    A(p3_x2sync)

                    def p3_x2syncv(v, sg_x2=sg_x2):
                        v.wait_ge(SEM["sg"], sg_x2)
                    V(p3_x2syncv)
                    sa_ln2 = ln_stats(x2, extra_ss=True)

                    def p3_h2(v, sa_ln2=sa_ln2):
                        sop(v, "sv", lambda: v.tensor_scalar(h2b[:], x2[:], mu[:],
                                                             rstd[:], ALU.subtract,
                                                             ALU.mult),
                            also=[("sa", sa_ln2)])
                    V(p3_h2)
                    sv_h2b = C["sv"]

                    def p3_h2f(v):
                        sop(v, "sv", lambda: v.tensor_scalar(h2f[:], x2[:], mu[:],
                                                             rstd[:], ALU.subtract,
                                                             ALU.mult))
                    V(p3_h2f)

                    for c in range(16):
                        pt = [ptr4, ptr5][c % 2]

                        def p3_tr(t, c=c, pt=pt, sv_h2b=sv_h2b, u=u):
                            t.wait_ge(SEM["sv"], sv_h2b)
                            if p3tr_ev[c % 2] is not None:
                                t.wait_ge(SEM[p3tr_ev[c % 2][0]], p3tr_ev[c % 2][1])
                            t.transpose(pt[:, 0:128], h2b[:, c * 128:(c + 1) * 128],
                                        id_s[:]).then_inc(SEM["pms"], 1)
                            C["pms"] += 1
                        T(p3_tr)
                        pms_t = C["pms"]

                        if c % 2 == 0:
                            def p3_trev(v, c=c, pt=pt, pms_t=pms_t):
                                sop(v, "sv", lambda: v.tensor_copy(
                                    h2T[:, c * 128:(c + 1) * 128], pt[:, 0:128]),
                                    also=[("pms", pms_t)])
                            V(p3_trev)
                            p3tr_ev[c % 2] = ("sv", C["sv"])
                        else:
                            def p3_trev(a, c=c, pt=pt, pms_t=pms_t):
                                sop(a, "sa", lambda: a.activation(
                                    h2T[:, c * 128:(c + 1) * 128], pt[:, 0:128],
                                    AF.Identity), also=[("pms", pms_t)])
                            A(p3_trev)
                            p3tr_ev[c % 2] = ("sa", C["sa"])
                    sv_h2T = C["sv"]
                    sa_h2T = C["sa"]

                    def p3_mm1(t, u=u, sv_h2T=sv_h2T, sa_h2T=sa_h2T):
                        t.wait_ge(SEM["sv"], sv_h2T)
                        t.wait_ge(SEM["sa"], sa_h2T)
                        if (u - 1) in pf_evac:
                            t.wait_ge(SEM["sv"], pf_evac[u - 1])
                        for c in range(16):
                            mi = t.matmul(pf[:, 0:136],
                                          h2T[:, c * 128:(c + 1) * 128],
                                          wcat_s[:, c * 144:c * 144 + 136],
                                          start=(c == 0), stop=(c == 15))
                        mi.then_inc(SEM["pm"], 1)
                        C["pm"] += 1
                    T(p3_mm1)
                    pm_mm1 = C["pm"]

                    def p3_gate1(v, u=u, pm_mm1=pm_mm1):
                        sop(v, "sv", lambda: v.tensor_sub(Dc[:], pf[:, 0:128], ccu_b[:]),
                            also=[("pm", pm_mm1)])
                        sop(v, "sv", lambda: v.tensor_mul(D2t[:], Dc[:, 0:64], Dc[:, 0:64]))
                        sop(v, "sv", lambda: v.reduce_sum(
                            sco[:], D2t[:].rearrange("p (n s) -> p n s", n=8), axis=AX.X))
                        sop(v, "sv", lambda: v.tensor_scalar(nd[:], pf[:, 128:136],
                                                             -2.0, ssh2[:], ALU.mult,
                                                             ALU.add))
                        pf_evac[u] = C["sv"]
                        sop(v, "sv", lambda: v.tensor_add(nd[:], nd[:], mm_b[:]))
                        sop(v, "sv", lambda: v.reciprocal(nd[:], nd[:]))
                        sop(v, "sv", lambda: v.tensor_mul(sco[:], sco[:], nd[:]))
                        sop(v, "sv", lambda: v.tensor_sub(zt[:], sco[:], center_s[:]))
                        sop(v, "sv", lambda: v.tensor_mul(zt[:], zt[:], slope_s[:]))
                        sop(v, "sv", lambda: v.tensor_scalar(zt[:], zt[:], -80.0,
                                                             None, ALU.max))
                    V(p3_gate1)
                    sv_z = C["sv"]

                    def p3_gate2(a, sv_z=sv_z):
                        sop(a, "sa", lambda: a.activation(gt[:], zt[:], AF.Exp,
                                                          scale=np.float32(-1.0)),
                            also=[("sv", sv_z)])
                    A(p3_gate2)
                    sa_g = C["sa"]

                    def p3_gate3(v, sa_g=sa_g):
                        sop(v, "sv", lambda: v.tensor_scalar(gt[:], gt[:], 1.0, None,
                                                             ALU.add),
                            also=[("sa", sa_g)])
                        sop(v, "sv", lambda: v.reciprocal(gt[:], gt[:]))
                        sop(v, "sv", lambda: v.reduce_max(mx[:], gt[:], axis=AX.X))
                        sop(v, "sv", lambda: v.tensor_scalar(eq[:], gt[:], mx[:],
                                                             None, ALU.is_ge))
                        sop(v, "sv", lambda: v.tensor_copy(t8a[:, 0:1], eq[:, 0:1]))
                        sop(v, "sv", lambda: v.tensor_add(t8a[:, 1:8], eq[:, 1:8],
                                                          eq[:, 0:7]))
                        sop(v, "sv", lambda: v.tensor_copy(t8b[:, 0:2], t8a[:, 0:2]))
                        sop(v, "sv", lambda: v.tensor_add(t8b[:, 2:8], t8a[:, 2:8],
                                                          t8a[:, 0:6]))
                        sop(v, "sv", lambda: v.tensor_copy(t8a[:, 0:4], t8b[:, 0:4]))
                        sop(v, "sv", lambda: v.tensor_add(t8a[:, 4:8], t8b[:, 4:8],
                                                          t8b[:, 0:4]))
                        sop(v, "sv", lambda: v.tensor_scalar(oh[:], t8a[:], 1.5,
                                                             None, ALU.is_le))
                        sop(v, "sv", lambda: v.tensor_mul(oh[:], oh[:], eq[:]))
                        sop(v, "sv", lambda: v.tensor_mul(
                            M2[:, 0:64].rearrange("p (n k) -> p n k", n=8),
                            Dc[:, 64:128].rearrange("p (n k) -> p n k", n=8),
                            oh[:].unsqueeze(2).broadcast_to([128, 8, 8])))
                        sop(v, "sv", lambda: v.tensor_copy(M2[:, 64:72], oh[:]))
                        sop(v, "sv", lambda: v.scalar_tensor_tensor(
                            thg[:], h2f[:], mx[:], h2f[:], ALU.mult, ALU.subtract))
                    V(p3_gate3)
                    sv_m2 = C["sv"]

                    def p3_m2t(t, sv_m2=sv_m2, u=u):
                        t.wait_ge(SEM[p3tr_ev[0][0]], p3tr_ev[0][1])
                        t.wait_ge(SEM["sv"], sv_m2)
                        t.transpose(ptr4[0:72, 0:128], M2[:, 0:72], id_s[:]
                                    ).then_inc(SEM["pms"], 1)
                        C["pms"] += 1
                    T(p3_m2t)
                    pms_m2 = C["pms"]

                    def p3_m2tev(v, pms_m2=pms_m2):
                        sop(v, "sv", lambda: v.tensor_copy(
                            M2T[:], ptr4[0:72, 0:128]),
                            also=[("pms", pms_m2)])
                        p3tr_ev[0] = ("sv", C["sv"])
                    V(p3_m2tev)
                    sv_m2T = C["sv"]

                    def p3_mm2(t, u=u, sv_m2T=sv_m2T):
                        t.wait_ge(SEM["sv"], sv_m2T)
                        for fc in range(4):
                            t.wait_ge(SEM["sv"], po_evac[(u, fc)])
                            mi = t.matmul(pob[fc][:], M2T[:],
                                          g_s[:, fc * 512:(fc + 1) * 512],
                                          start=True, stop=True)
                        mi.then_inc(SEM["pm"], 1)
                        C["pm"] += 1
                    T(p3_mm2)
                    pm_mm2 = C["pm"]

                    for fc in range(4):
                        def p3_ffnout(v, u=u, fc=fc, pm_mm2=pm_mm2):
                            also = [("pm", pm_mm2)] if fc == 0 else []
                            sop(v, "sv", lambda: v.scalar_tensor_tensor(
                                ffn[:, fc * 512:(fc + 1) * 512], pob[fc][:], mx[:],
                                thg[:, fc * 512:(fc + 1) * 512], ALU.mult,
                                ALU.subtract), also=also)
                            po_evac[(u, fc)] = C["sv"]
                        V(p3_ffnout)

                        def p3_fch(a, u=u, fc=fc):
                            sop(a, "sa", lambda: a.activation(
                                h2b[:, fc * 512:(fc + 1) * 512],
                                ffn[:, fc * 512:(fc + 1) * 512], AF.Square,
                                accum_out=ssq4[:, fc:fc + 1]),
                                also=[("sv", po_evac[(u, fc)])])
                            sop(a, "sa", lambda: a.activation(
                                h2b[:, fc * 512:(fc + 1) * 512],
                                ffn[:, fc * 512:(fc + 1) * 512], AF.Identity,
                                accum_out=s14[:, fc:fc + 1]))
                        A(p3_fch)
                    ffn_evac[u] = po_evac[(u, 3)]
                    sa_ln3 = ln_combine(C["sa"])

                    def p3_out(v, sa_ln3=sa_ln3):
                        sop(v, "sv", lambda: v.tensor_scalar(thg[:], ffn[:], mu[:],
                                                             rstd[:], ALU.subtract,
                                                             ALU.mult),
                            also=[("sa", sa_ln3)])
                    V(p3_out)
                    sv_thg3 = C["sv"]

                    def p3_outg(g, sv_thg3=sv_thg3, u=u):
                        also = [("sv", sv_thg3)]
                        if u >= 1:
                            also.append(("ost", 16 * u))
                        sop(g, "sg", lambda: g.tensor_add(ot[:], x2[:], thg[:]),
                            also=also)
                    G(p3_outg)
                    p3_done.append((C["sv"], C["sg"]))

                    def p3_store(s, u=u):
                        s.wait_ge(SEM["sg"], p3_done[u][1])
                        s.dma_start(out=y[u * 128:(u + 1) * 128, :], in_=ot[:]
                                    ).then_inc(SEM["ost"], 16)
                        C["ost"] += 16
                    SY(p3_store)

                if taps:
                    def p3_taps(s):
                        s.wait_ge(SEM["ost"], C["ost"])
                        if "qkv" in tap_out:
                            s.dma_start(out=tap_out["qkv"], in_=qkv_raw
                                        ).then_inc(SEM["ost"], 16)
                            C["ost"] += 16

                        if "at" in tap_out:
                            s.dma_start(out=tap_out["at"], in_=arenaB[:]
                                        ).then_inc(SEM["ost"], 16)
                            C["ost"] += 16
                    SY(p3_taps)

                def p3_fin(s):
                    s.wait_ge(SEM["ost"], C["ost"])
                SY(p3_fin)

    return nc


def _host_constants():
    inv_freq = (10000.0 ** (-(np.arange(0, D, 2, dtype=np.float32) / np.float32(D)))
                ).astype(np.float32)
    emb = np.repeat(inv_freq, 2)
    arg = (np.float32(8192.0) * emb).astype(np.float32)
    sinb = np.tile(np.sin(arg).astype(np.float32)[None, :], (128, 1))
    cosb = np.tile(np.cos(arg).astype(np.float32)[None, :], (128, 1))
    return np.ascontiguousarray(sinb), np.ascontiguousarray(cosb)


def make_in_maps(x, Wqkv, bqkv, Wout, bout, sw, md, ic, isl, lu, ld, bw, dw):
    WqkvT = np.ascontiguousarray(Wqkv.T)
    WoutT = np.ascontiguousarray(Wout.T)
    sinb, cosb = _host_constants()
    sin2 = np.ascontiguousarray(np.concatenate([sinb, sinb], axis=1))
    cos2 = np.ascontiguousarray(np.concatenate([cosb, cosb], axis=1))
    identb = np.eye(128, dtype=np.float32).astype(ml_dtypes.bfloat16)
    onesb = np.ones((128, 128), dtype=np.float32).astype(ml_dtypes.bfloat16)
    Wcat = np.ascontiguousarray(np.concatenate([
        np.transpose(sw, (1, 0, 2)).reshape(D, 64),
        np.transpose(lu, (1, 0, 2)).reshape(D, 64),
        md.T, dw.T], axis=1), dtype=np.float32)
    Gmat = np.ascontiguousarray(np.concatenate(
        [np.transpose(ld, (0, 2, 1)).reshape(64, D), bw], axis=0), dtype=np.float32)
    slope_t = np.ascontiguousarray(np.tile(isl[None, :], (128, 1)).astype(np.float32))
    center_t = np.ascontiguousarray(np.tile(ic[None, :], (128, 1)).astype(np.float32))

    in_maps = []
    for core in range(8):
        b, r = core // 4, core % 4
        u = np.arange(128)
        w = np.arange(128)
        mask4 = np.stack([(4 * w[:, None] + o) <= (4 * u[None, :] + r)
                          for o in range(4)])
        masks = mask4.astype(np.float32).astype(ml_dtypes.bfloat16)
        in_maps.append(dict(
            x_sh=np.ascontiguousarray(x[b, r::4, :]), WqkvT=WqkvT,
            bqkv=np.ascontiguousarray(bqkv[None, :]), WoutTd=WoutT,
            bout=np.ascontiguousarray(bout[None, :]), sin2d=sin2, cos2d=cos2,
            masksd=np.ascontiguousarray(masks), identd=identb, onesd=onesb,
            WcatKd=Wcat, Gmatd=Gmat, slopd=slope_t, centd=center_t,
        ))
    return in_maps


def kernel(L0_attn_x, L0_attn_Wqkv, L0_attn_bqkv, L0_attn_Wout, L0_attn_bout,
           L0_ffn_select_weight, L0_ffn_select_mean_diff, L0_ffn_imp_center,
           L0_ffn_imp_slope, L0_ffn_lora_update, L0_ffn_lora_degen,
           L0_ffn_bias_w, L0_ffn_debias_w):
    args = [np.asarray(a, dtype=np.float32) for a in
            (L0_attn_x, L0_attn_Wqkv, L0_attn_bqkv, L0_attn_Wout, L0_attn_bout,
             L0_ffn_select_weight, L0_ffn_select_mean_diff, L0_ffn_imp_center,
             L0_ffn_imp_slope, L0_ffn_lora_update, L0_ffn_lora_degen,
             L0_ffn_bias_w, L0_ffn_debias_w)]
    in_maps = make_in_maps(*args)
    nc = build_program(DEBUG_TAPS)
    res = run_bass_kernel_spmd(nc, in_maps, core_ids=list(range(8)))
    out = np.empty((2, 8192, D), dtype=np.float32)
    for core in range(8):
        b, r = core // 4, core % 4
        out[b, r::4, :] = res.results[core]["y"]
    if DEBUG_TAPS:
        kernel.tap_results = res.results
    return out


if __name__ == "__main__":
    nc = build_program()
    n = sum(1 for _ in nc.all_instructions())
    print("instructions:", n)
